# version 02 (final)
# baseline (speedup 1.0000x reference)
import numpy as np
import ml_dtypes
from contextlib import ExitStack

import concourse.bass as bass
import concourse.mybir as mybir
from concourse.bass_utils import run_bass_kernel_spmd

F32 = mybir.dt.float32
BF16 = mybir.dt.bfloat16
I32 = mybir.dt.int32
AF = mybir.ActivationFunctionType
ALU = mybir.AluOpType
AX = mybir.AxisListType

D = 2048
FF = 8192
B = 2
S = 4096
SL = 1024
H = 32
HD = 64
NC = 8
EPS_LN = 1e-5
EPS_Q = 1e-5
WN = float(D * FF)


def core_blocks(c):
    j = c % 4
    return [2 * j, 2 * j + 1]


def own_of_block(b):
    return b // 2


def off_of_block(b):
    return 512 * (b % 2)


PREP_NAMES = ["g", "u", "d", "w1", "w2"]
PREP_SHAPES = {"g": (D, SL), "u": (D, SL), "d": (SL, D), "w1": (D, SL), "w2": (SL, D)}


import os
PREP_MODE = os.environ.get('PREP_MODE', 'full')


def build_prep():
    nc = bass.Bass("TRN2", target_bir_lowering=False, detect_race_conditions=False)
    ins = {n: nc.dram_tensor("w_" + n, list(PREP_SHAPES[n]), F32, kind="ExternalInput") for n in PREP_NAMES}
    outs = {n: nc.dram_tensor("q_" + n, list(PREP_SHAPES[n]), F32, kind="ExternalOutput") for n in PREP_NAMES}
    gout = nc.dram_tensor("gammas", [1, 8], F32, kind="ExternalOutput")
    ar_in = nc.dram_tensor("ar_in", [1, 8], F32)
    ar_out = nc.dram_tensor("ar_out", [1, 8], F32)
    invg_d = nc.dram_tensor("invg_d", [1, 8], F32)
    r0_d = nc.dram_tensor("r0_d", [1, 8], F32)
    g_d = nc.dram_tensor("g_d", [1, 8], F32)

    CH = 1024
    chunks = []
    for n in PREP_NAMES:
        R, C = PREP_SHAPES[n]
        for r0 in range(0, R, 128):
            for c0 in range(0, C, CH):
                chunks.append((n, r0, c0))
    nch = len(chunks)
    per_w = nch // 5

    def Q(i):
        return nch + 6 + i

    with ExitStack() as st:
        stage = [st.enter_context(nc.sbuf_tensor(f"stage{i}", [128, CH], F32)) for i in range(2)]
        qtile = [st.enter_context(nc.sbuf_tensor(f"qtile{i}", [128, CH], F32)) for i in range(2)]
        itile = st.enter_context(nc.sbuf_tensor("itile", [128, CH], I32))
        ftile = st.enter_context(nc.sbuf_tensor("ftile", [128, CH], F32))
        qf = st.enter_context(nc.sbuf_tensor("qf", [128, CH], F32))
        wcols = st.enter_context(nc.sbuf_tensor("wcols", [128, 16], F32))
        ones_col = st.enter_context(nc.sbuf_tensor("ones_col", [128, 1], F32))
        ar_row = st.enter_context(nc.sbuf_tensor("ar_row", [1, 8], F32))
        ones_row = st.enter_context(nc.sbuf_tensor("ones_row", [1, 128], F32))
        prow = st.enter_context(nc.sbuf_tensor("prow", [1, 8], F32))
        grow = st.enter_context(nc.sbuf_tensor("grow", [1, 8], F32))
        invg = st.enter_context(nc.sbuf_tensor("invg", [1, 128], F32))
        rtmp = st.enter_context(nc.sbuf_tensor("rtmp", [1, 8], F32))
        r0b = st.enter_context(nc.sbuf_tensor("r0b", [1, 8], F32))
        growb = st.enter_context(nc.sbuf_tensor("growb", [1, 8], F32))
        nt = st.enter_context(nc.sbuf_tensor("nt", [1, 32], F32))
        invg_col = st.enter_context(nc.sbuf_tensor("invg_col", [128, 8], F32))
        psumw = [st.enter_context(nc.psum_tensor(f"psumw{i}", [128, 512], F32)) for i in range(5)]
        psumb = st.enter_context(nc.psum_tensor("psumb", [128, 512], F32))
        s_ld = [st.enter_context(nc.semaphore(f"s_ld{i}")) for i in range(2)]
        s_st = [st.enter_context(nc.semaphore(f"s_st{i}")) for i in range(2)]
        s_sta = st.enter_context(nc.semaphore("s_sta"))
        s_stb = st.enter_context(nc.semaphore("s_stb"))
        s_stc = st.enter_context(nc.semaphore("s_stc"))
        s_ve = st.enter_context(nc.semaphore("s_ve"))
        s_pe = st.enter_context(nc.semaphore("s_pe"))
        s_cc = st.enter_context(nc.semaphore("s_cc"))

        with nc.Block() as blk:
            @blk.gpsimd
            def _(g):
                for i in range(nch):
                    n, r0, c0 = chunks[i]
                    if i >= 2:
                        g.wait_ge(s_ve, i - 1)
                    g.dma_start(stage[i % 2][:], ins[n][r0:r0 + 128, c0:c0 + CH]).then_inc(s_ld[i % 2], 16)
                g.wait_ge(s_sta, 16)
                if PREP_MODE == 'nocc':
                    g.dma_start(ar_out[:, :], ar_in[:, :]).then_inc(s_cc, 1)
                else:
                    g.collective_compute(
                        "AllReduce", ALU.add,
                        replica_groups=[list(range(NC))],
                        ins=[ar_in.ap().opt()],
                        outs=[ar_out.ap().opt()],
                    ).then_inc(s_cc, 1)
                for i in range(nch):
                    n, r0, c0 = chunks[i]
                    g.wait_ge(s_ve, max(nch, Q(i - 2)) if i >= 2 else nch)
                    g.dma_start(stage[i % 2][:], ins[n][r0:r0 + 128, c0:c0 + CH]).then_inc(s_ld[i % 2], 16)

            @blk.vector
            def _(v):
                v.memset(ones_col[:], 1.0)
                v.memset(ones_row[:], 1.0)
                v.memset(ar_row[:], 0.0)
                v.memset(invg[0:1, 8:128], 1.0)
                for i in range(nch):
                    v.wait_ge(s_ld[i % 2], 16 * (i // 2 + 1))
                    widx = i // per_w
                    if widx >= 1:
                        v.wait_ge(s_pe, widx)
                    v.reduce_sum(wcols[:, i % per_w:i % per_w + 1], stage[i % 2][:], axis=AX.X,
                                 apply_absolute_value=True).then_inc(s_ve, 1)
                v.wait_ge(s_pe, 5)
                for widx in range(5):
                    ins_r = v.reduce_sum(ar_row[0:1, widx:widx + 1], psumw[widx][0:1, 0:16],
                                         axis=AX.X)
                ins_r.then_inc(s_ve, 1)
                v.wait_ge(s_stb, 16)
                v.tensor_scalar_mul(grow[:], grow[:], 1.0 / WN)
                v.memset(grow[0:1, 5:8], 1.0)
                v.tensor_scalar_max(grow[0:1, 0:3], grow[0:1, 0:3], EPS_Q)
                v.tensor_scalar_add(grow[0:1, 3:5], grow[0:1, 3:5], EPS_Q).then_inc(s_ve, 1)
                v.wait_ge(s_stb, 48)
                v.reciprocal(invg[0:1, 0:8], growb[:]).then_inc(s_ve, 1)
                v.wait_ge(s_stb, 80)
                v.tensor_mul(nt[0:1, 0:8], r0b[:], r0b[:])
                v.tensor_scalar_mul(nt[0:1, 8:16], r0b[:], 2.0)
                v.tensor_mul(nt[0:1, 16:24], nt[0:1, 0:8], growb[:])
                v.memset(nt[0:1, 24:32], 0.0)
                v.tensor_sub(invg[0:1, 0:8], nt[0:1, 8:16], nt[0:1, 16:24]).then_inc(s_ve, 1)
                v.wait_ge(s_stb, 112)
                v.tensor_copy(invg_col[:], invg_col[:]).then_inc(s_ve, 1)
                for i in range(nch):
                    n, r0, c0 = chunks[i]
                    widx = PREP_NAMES.index(n)
                    v.wait_ge(s_ld[i % 2], 16 * (nch // 2 + i // 2 + 1))
                    if i >= 2:
                        v.wait_ge(s_st[i % 2], 16 * (i // 2))
                    v.tensor_scalar(stage[i % 2][:], stage[i % 2][:], invg_col[:, widx:widx + 1], 0.5,
                                    ALU.mult, ALU.add)
                    v.tensor_copy(itile[:], stage[i % 2][:])
                    v.tensor_copy(ftile[:], itile[:])
                    v.tensor_tensor(qf[:], stage[i % 2][:], ftile[:], ALU.is_lt)
                    v.tensor_tensor(qf[:], ftile[:], qf[:], ALU.subtract)
                    v.tensor_scalar(qtile[i % 2][:], qf[:], 1.0, -1.0, ALU.min, ALU.max).then_inc(s_ve, 1)

            @blk.tensor
            def _(t):
                for widx in range(5):
                    t.wait_ge(s_ve, per_w * (widx + 1))
                    t.matmul(psumw[widx][0:1, 0:16], ones_col[:], wcols[:],
                             start=True, stop=True).then_inc(s_pe, 1)


            @blk.sync
            def _(sy):
                sy.wait_ge(s_ve, nch + 1)
                sy.dma_start(ar_in[:, :], ar_row[:]).then_inc(s_sta, 16)
                sy.wait_ge(s_cc, 1)
                sy.dma_start(grow[:], ar_out[:, :]).then_inc(s_stb, 16)
                sy.wait_ge(s_ve, nch + 2)
                sy.dma_start(g_d[:, :], grow[:]).then_inc(s_stb, 16)
                sy.wait_ge(s_stb, 32)
                sy.dma_start(growb[:], g_d[:, :]).then_inc(s_stb, 16)
                sy.wait_ge(s_ve, nch + 3)
                sy.dma_start(r0_d[:, :], invg[0:1, 0:8]).then_inc(s_stb, 16)
                sy.wait_ge(s_stb, 64)
                sy.dma_start(r0b[:], r0_d[:, :]).then_inc(s_stb, 16)
                sy.wait_ge(s_ve, nch + 4)
                sy.dma_start(invg_d[:, :], invg[0:1, 0:8]).then_inc(s_stb, 16)
                sy.wait_ge(s_stb, 96)
                sy.dma_start(invg_col[:], bass.AP(invg_d, 0, [[0, 128], [1, 8]])).then_inc(s_stb, 16)
                for i in range(nch):
                    n, r0, c0 = chunks[i]
                    sy.wait_ge(s_ve, Q(i))
                    sy.dma_start(outs[n][r0:r0 + 128, c0:c0 + CH], qtile[i % 2][:]).then_inc(s_st[i % 2], 16)
                sy.dma_start(gout[:, :], grow[:]).then_inc(s_stc, 16)
                sy.wait_ge(s_st[0], 16 * (nch // 2))
                sy.wait_ge(s_st[1], 16 * (nch // 2))
                sy.wait_ge(s_stc, 16)

    return nc


import os

L0_PHASE = os.environ.get("L0_PHASE", "all")
_PHN = {"ln1": 1, "proj": 2, "attn": 3, "ln2": 4, "all": 9}


class _St:

    def __init__(self, nc, stk, name):
        self.sems = [stk.enter_context(nc.semaphore(name + "0")),
                     stk.enter_context(nc.semaphore(name + "1"))]
        self.n = 0

    def issue(self, instr):
        instr.then_inc(self.sems[self.n % 2], 16)
        self.n += 1
        return self.n - 1

    def wait(self, eng, idx):
        if idx < 0:
            return
        eng.wait_ge(self.sems[idx % 2], 16 * (idx // 2 + 1))

    def wait_all(self, eng):
        self.wait(eng, self.n - 1)
        self.wait(eng, self.n - 2)


def build_l0(core):
    ph = _PHN[L0_PHASE]
    nc = bass.Bass("TRN2", target_bir_lowering=False, detect_race_conditions=False)
    xt = nc.dram_tensor("xt", [D, SL], F32, kind="ExternalInput")
    wq = nc.dram_tensor("wq", [D, D], F32, kind="ExternalInput")
    wk = nc.dram_tensor("wk", [D, D], F32, kind="ExternalInput")
    wv = nc.dram_tensor("wv", [D, D], F32, kind="ExternalInput")
    wo = nc.dram_tensor("wo", [D, D], F32, kind="ExternalInput")
    qg = nc.dram_tensor("qg", [D, FF], F32, kind="ExternalInput")
    qu = nc.dram_tensor("qu", [D, FF], F32, kind="ExternalInput")
    qd = nc.dram_tensor("qd", [FF, D], F32, kind="ExternalInput")
    gam = nc.dram_tensor("gam", [1, 8], F32, kind="ExternalInput")
    yt = nc.dram_tensor("yt", [D, SL], F32, kind="ExternalOutput")

    rx_d = nc.dram_tensor("rx_d", [D, SL], F32)
    kt_in = nc.dram_tensor("kt_in", [D, SL], BF16)
    kt_out = nc.dram_tensor("kt_out", [4 * D, SL], BF16)
    v_in = nc.dram_tensor("v_in", [SL, 2176], BF16)
    v_out = nc.dram_tensor("v_out", [S, 2176], BF16)
    row_d = nc.dram_tensor("row_d", [1, 4096], F32)
    smax_d = nc.dram_tensor("smax_d", [1, 1024], F32)
    am_d = nc.dram_tensor("am_d", [128, 1024], BF16)

    dbg = {}
    if ph < 9:
        dbg["a1"] = nc.dram_tensor("dbg_a1", [128, 16384], F32, kind="ExternalOutput")
        dbg["a2"] = nc.dram_tensor("dbg_a2", [128, 16384], F32, kind="ExternalOutput")
        dbg["a3"] = nc.dram_tensor("dbg_a3", [128, 32768], F32, kind="ExternalOutput")
        dbg["rx"] = nc.dram_tensor("dbg_rx", [D, SL], F32, kind="ExternalOutput")

    with ExitStack() as stk:
        A1 = stk.enter_context(nc.sbuf_tensor("A1", [128, 16384], BF16))
        A2 = stk.enter_context(nc.sbuf_tensor("A2", [128, 16384], BF16))
        A3 = stk.enter_context(nc.sbuf_tensor("A3", [128, 32768], BF16))
        SCR = stk.enter_context(nc.sbuf_tensor("SCR", [128, 9216], F32))
        RXT = [stk.enter_context(nc.sbuf_tensor(f"RXT{i}", [128, 512], F32)) for i in range(4)]
        STG = [stk.enter_context(nc.sbuf_tensor(f"STG{i}", [128, 512], BF16)) for i in range(4)]
        SQT = [stk.enter_context(nc.sbuf_tensor(f"SQT{i}", [128, 512], BF16)) for i in range(2)]
        ITL = stk.enter_context(nc.sbuf_tensor("ITL", [128, 512], I32))
        FCT = stk.enter_context(nc.sbuf_tensor("FCT", [128, 512], F32))
        ROW = stk.enter_context(nc.sbuf_tensor("ROW", [1, 4096], F32))
        VCOL = stk.enter_context(nc.sbuf_tensor("VCOL", [128, 8], F32))
        ONESC = stk.enter_context(nc.sbuf_tensor("ONESC", [128, 1], BF16))
        ONESF = stk.enter_context(nc.sbuf_tensor("ONESF", [128, 1], F32))
        EPSC = stk.enter_context(nc.sbuf_tensor("EPSC", [128, 1], F32))
        GCOL = stk.enter_context(nc.sbuf_tensor("GCOL", [128, 8], F32))
        C1T = stk.enter_context(nc.sbuf_tensor("C1T", [128, 128], BF16))
        AMB = stk.enter_context(nc.sbuf_tensor("AMB", [128, 1024], BF16))
        AMT = stk.enter_context(nc.sbuf_tensor("AMT", [128, 1024], BF16))
        PS = [stk.enter_context(nc.psum_tensor(f"PS{i}", [128, 512], F32)) for i in range(8)]

        s_pe = stk.enter_context(nc.semaphore("s_pe"))
        s_ve = stk.enter_context(nc.semaphore("s_ve"))
        s_se = stk.enter_context(nc.semaphore("s_se"))
        s_cc = stk.enter_context(nc.semaphore("s_cc"))
        LPW = _St(nc, stk, "lpw")
        LPK = _St(nc, stk, "lpk")
        LPR = _St(nc, stk, "lpr")
        SPS = _St(nc, stk, "sps")

        C = {"pe": 0, "ve": 0, "se": 0, "cc": 0}

        def pe(mi):
            mi.then_inc(s_pe, 1); C["pe"] += 1; return C["pe"]

        def ve(mi):
            mi.then_inc(s_ve, 1); C["ve"] += 1; return C["ve"]

        def se(mi):
            mi.then_inc(s_se, 1); C["se"] += 1; return C["se"]

        def cc(mi):
            mi.then_inc(s_cc, 1); C["cc"] += 1; return C["cc"]

        def XN(t, h):
            return A1[:, 1024 * t + 512 * h: 1024 * t + 512 * h + 512]

        def XNs(t, h, sc):
            return A1[:, 1024 * t + 512 * h + 128 * sc: 1024 * t + 512 * h + 128 * sc + 128]

        def PT(kc):
            return A1[:, 512 * kc:512 * kc + 512]

        def QT(e, h):
            return A2[:, 1024 * e + 512 * h: 1024 * e + 512 * h + 512]

        def OT(e):
            return A3[:, 17408 + 512 * e:17408 + 512 * e + 512]

        def HT(fc):
            return A3[:, 512 * fc:512 * fc + 512]

        def SLOT(i):
            return SCR[:, 512 * i:512 * i + 512]
        KCB = [A3[:, 0:2048], A3[:, 2048:4096]]
        VCB = [A3[:, 4096:8192], A3[:, 8192:12288]]
        WSB = [A3[:, 12288:14336], A3[:, 14336:16384]]
        WVB = [A3[:, 16384:16896], A3[:, 16896:17408]]
        QGB = [A2[:, 0:2048], A2[:, 2048:4096]]
        QUB = [A2[:, 4096:6144], A2[:, 6144:8192]]
        QDB = [A2[:, 0:8192], A2[:, 8192:16384]]

        def kslice(kc):
            b = kc // 4
            own = b if b < 4 else 7 - b
            off = (0 if b < 4 else 512) + (kc % 4) * 128
            return own, off

        with nc.Block() as blk:
            @blk.sync
            def _(sy):
                LPK.issue(sy.dma_start(GCOL[:], bass.AP(gam, 0, [[0, 128], [1, 8]])))

            @blk.gpsimd
            def _(g):
                SPS.issue(g.dma_start(rx_d[:, :], xt[:, :]))

            @blk.vector
            def _(v):
                v.memset(ONESC[:], 1.0)
                v.memset(ONESF[:], 1.0)
                v.memset(EPSC[:], EPS_LN)
                v.memset(C1T[:], 0.0)
                ve(v.memset(C1T[:, 0:1], 1.0))

        def emit_ln(quant):
            rx_ready = SPS.n - 1
            with nc.Block() as blk:
                ld0 = LPR.n
                sbase = C["se"]
                pebase = C["pe"]

                @blk.gpsimd
                def _(g):
                    SPS.wait(g, rx_ready)
                    SPS.wait(g, rx_ready - 1)
                    for i in range(32):
                        if i >= 4:
                            g.wait_ge(s_pe, pebase + i - 3)
                            g.wait_ge(s_se, sbase + i - 3)
                        t, h = i % 16, i // 16
                        LPR.issue(g.dma_start(RXT[i % 4][:],
                                              rx_d[128 * t:128 * t + 128, 512 * h:512 * h + 512]))

                @blk.scalar
                def _(s):
                    for i in range(32):
                        LPR.wait(s, ld0 + i)
                        if i >= 2:
                            s.wait_ge(s_pe, pebase + i - 1)
                        se(s.activation(SQT[i % 2][:], RXT[i % 4][:], AF.Square))

                @blk.tensor
                def _(t_):
                    for i in range(32):
                        t, h = i % 16, i // 16
                        LPR.wait(t_, ld0 + i)
                        t_.wait_ge(s_se, sbase + i + 1)
                        t_.matmul(PS[2 * h][0:1, :], ONESF[:], RXT[i % 4][:],
                                  start=(t == 0), stop=(t == 15))
                        mi = t_.matmul(PS[2 * h + 1][0:1, :], ONESC[:], SQT[i % 2][:],
                                       start=(t == 0), stop=(t == 15))
                        pe(mi)
                    mi = t_.matmul(PS[4][:, :], C1T[:], STG[1][:], start=True, stop=True)
                    pe(mi)

            with nc.Block() as blk:
                @blk.vector
                def _(v):
                    v.wait_ge(s_pe, C["pe"])
                    v.tensor_copy(ROW[0:1, 0:512], PS[0][0:1, :])
                    v.tensor_copy(ROW[0:1, 512:1024], PS[1][0:1, :])
                    v.tensor_copy(ROW[0:1, 1024:1536], PS[2][0:1, :])
                    ve(v.tensor_copy(ROW[0:1, 1536:2048], PS[3][0:1, :]))

                @blk.sync
                def _(sy):
                    sy.wait_ge(s_ve, C["ve"])
                    i0 = LPK.issue(sy.dma_start(row_d[0:1, 0:2048], ROW[0:1, 0:2048]))
                    LPK.wait(sy, i0)
                    for h in range(2):
                        LPK.issue(sy.dma_start(SLOT(0 + 7 * h)[:],
                                               bass.AP(row_d, 1024 * h, [[0, 128], [1, 512]])))
                        LPK.issue(sy.dma_start(SLOT(1 + 7 * h)[:],
                                               bass.AP(row_d, 1024 * h + 512, [[0, 128], [1, 512]])))

            with nc.Block() as blk:
                @blk.vector
                def _(v):
                    LPK.wait_all(v)
                    for h in range(2):
                        v.tensor_scalar_mul(SLOT(2 + 7 * h)[:], SLOT(0 + 7 * h)[:], 1.0 / D)
                        v.tensor_scalar_mul(SLOT(3 + 7 * h)[:], SLOT(1 + 7 * h)[:], 1.0 / D)
                    for h in range(2):
                        v.tensor_mul(SLOT(0 + 7 * h)[:], SLOT(2 + 7 * h)[:], SLOT(2 + 7 * h)[:])
                    for h in range(2):
                        ve(v.tensor_sub(SLOT(4 + 7 * h)[:], SLOT(3 + 7 * h)[:], SLOT(0 + 7 * h)[:]))

                @blk.scalar
                def _(s):
                    s.wait_ge(s_ve, C["ve"])
                    for h in range(2):
                        se(s.activation(SLOT(3 + 7 * h)[:], SLOT(4 + 7 * h)[:], AF.Sqrt, bias=EPSC[:]))

                @blk.vector
                def _(v):
                    v.wait_ge(s_se, C["se"])
                    for h in range(2):
                        v.tensor_scalar_add(SLOT(4 + 7 * h)[:], SLOT(4 + 7 * h)[:], EPS_LN)
                    for h in range(2):
                        v.reciprocal(SLOT(5 + 7 * h)[:], SLOT(3 + 7 * h)[:])
                    for step in range(2):
                        for h in range(2):
                            v.tensor_mul(SLOT(14)[:] if h == 0 else SLOT(15)[:],
                                         SLOT(5 + 7 * h)[:], SLOT(5 + 7 * h)[:])
                        for h in range(2):
                            t = SLOT(14) if h == 0 else SLOT(15)
                            v.tensor_mul(t[:], t[:], SLOT(4 + 7 * h)[:])
                        for h in range(2):
                            t = SLOT(14) if h == 0 else SLOT(15)
                            v.tensor_scalar(t[:], t[:], -0.5, 1.5, ALU.mult, ALU.add)
                        for h in range(2):
                            t = SLOT(14) if h == 0 else SLOT(15)
                            v.tensor_mul(SLOT(5 + 7 * h)[:], SLOT(5 + 7 * h)[:], t[:])
                    for h in range(2):
                        ve(v.tensor_mul(SLOT(6 + 7 * h)[:], SLOT(2 + 7 * h)[:], SLOT(5 + 7 * h)[:]))

            with nc.Block() as blk:
                ld1 = LPR.n
                vbase2 = C["ve"]
                kv = 2 if quant else 1

                @blk.gpsimd
                def _(g):
                    for i in range(32):
                        if i >= 4:
                            g.wait_ge(s_ve, vbase2 + kv * (i - 3))
                        t, h = i % 16, i // 16
                        LPR.issue(g.dma_start(RXT[i % 4][:],
                                              rx_d[128 * t:128 * t + 128, 512 * h:512 * h + 512]))

                sebase2 = C["se"]

                @blk.vector
                def _(v):
                    if quant:
                        v.memset(AMB[:], 0.0)
                    for i in range(32):
                        t, h = i % 16, i // 16
                        LPR.wait(v, ld1 + i)
                        v.tensor_mul(SLOT(14)[:], RXT[i % 4][:], SLOT(5 + 7 * h)[:])
                        ve(v.tensor_sub(XN(t, h), SLOT(14)[:], SLOT(6 + 7 * h)[:]))
                        if quant:
                            v.wait_ge(s_se, sebase2 + i + 1)
                            ve(v.tensor_max(AMB[:, 512 * h:512 * h + 512],
                                            AMB[:, 512 * h:512 * h + 512], SQT[i % 2][:]))

                if quant:
                    @blk.scalar
                    def _(s):
                        for i in range(32):
                            t, h = i % 16, i // 16
                            s.wait_ge(s_ve, vbase2 + 2 * i + 1)
                            if i >= 2:
                                s.wait_ge(s_ve, vbase2 + 2 * i - 2)
                            se(s.activation(SQT[i % 2][:], XN(t, h), AF.Abs))

            if not quant:
                return
            emit_tokscale(src_amb_cols=1024, u_slots=(0, 7), sb_slots=(1, 8), cu_slots=(2, 9))
            with nc.Block() as blk:
                @blk.vector
                def _(v):
                    LPK.wait_all(v)
                    for i in range(32):
                        t, h = i % 16, i // 16
                        v.tensor_mul(SLOT(14)[:], XN(t, h), SLOT((1, 8)[h])[:])
                        v.tensor_scalar_add(SLOT(14)[:], SLOT(14)[:], 0.5)
                        v.tensor_copy(ITL[:], SLOT(14)[:])
                        v.tensor_copy(FCT[:], ITL[:])
                        v.tensor_sub(SLOT(15)[:], FCT[:], SLOT(14)[:])
                        v.tensor_scalar(SLOT(15)[:], SLOT(15)[:], 1e30, 1.0, ALU.mult, ALU.min)
                        v.tensor_scalar_max(SLOT(15)[:], SLOT(15)[:], 0.0)
                        ve(v.tensor_sub(XN(t, h), FCT[:], SLOT(15)[:]))

        def emit_tokscale(src_amb_cols, u_slots, sb_slots, cu_slots):
            nb = src_amb_cols // 128
            with nc.Block() as blk:
                @blk.gpsimd
                def _(g):
                    g.wait_ge(s_ve, C["ve"])
                    SPS.issue(g.dma_start(am_d[:, 0:src_amb_cols], AMB[:, 0:src_amb_cols]))

                @blk.sync
                def _(sy):
                    SPS.wait(sy, SPS.n - 1)
                    for b in range(nb):
                        LPK.issue(sy.dma_start(AMT[:, 128 * b:128 * b + 128],
                                               am_d[:, 128 * b:128 * b + 128], transpose=True))

            with nc.Block() as blk:
                ldb = LPK.n - nb

                @blk.vector
                def _(v):
                    for b in range(nb):
                        LPK.wait(v, ldb + b)
                        ve(v.reduce_max(VCOL[:, b:b + 1], AMT[:, 128 * b:128 * b + 128], axis=AX.X))

                @blk.gpsimd
                def _(g):
                    g.wait_ge(s_ve, C["ve"])
                    for b in range(nb):
                        SPS.issue(g.dma_start(smax_d[0:1, 128 * b:128 * b + 128], VCOL[:, b:b + 1]))

                @blk.sync
                def _(sy):
                    for b in range(nb):
                        SPS.wait(sy, SPS.n - nb + b)
                    for h in range(nb // 4):
                        LPK.issue(sy.dma_start(SLOT(u_slots[h])[:],
                                               bass.AP(smax_d, 512 * h, [[0, 128], [1, 512]])))

            with nc.Block() as blk:
                @blk.vector
                def _(v):
                    LPK.wait_all(v)
                    hs = range(nb // 4)
                    for h in hs:
                        v.tensor_scalar_max(SLOT(u_slots[h])[:], SLOT(u_slots[h])[:], EPS_Q)
                    for h in hs:
                        v.reciprocal(SLOT(sb_slots[h])[:], SLOT(u_slots[h])[:])
                    for h in hs:
                        v.tensor_scalar_mul(SLOT(sb_slots[h])[:], SLOT(sb_slots[h])[:], 127.0)
                    for h in hs:
                        ve(v.tensor_scalar_mul(SLOT(cu_slots[h])[:], SLOT(u_slots[h])[:], 1.0 / 127.0))

        def emit_projq(wsrc, dst_is_kt):
            with nc.Block() as blk:
                ld0 = LPW.n
                sbase = C["se"]
                peb = C["pe"]
                spsb = SPS.n

                @blk.gpsimd
                def _(g):
                    for e in range(16):
                        if dst_is_kt and e >= 1:
                            for hh in range(2):
                                k = 2 * (e - 1) + hh
                                g.wait_ge(s_se, sbase + k + 1)
                                SPS.issue(g.dma_start(kt_in[128 * (e - 1):128 * (e - 1) + 128,
                                                            512 * hh:512 * hh + 512],
                                                      STG[k % 4][:]))
                        if e >= 2:
                            g.wait_ge(s_se, sbase + 2 * (e - 1))
                        LPW.issue(g.dma_start(
                            WSB[e % 2].rearrange("p (t e) -> p t e", t=16),
                            wsrc.rearrange("(t p) e -> p t e", p=128)[:, :, 128 * e:128 * e + 128]))
                    if dst_is_kt:
                        for hh in range(2):
                            k = 30 + hh
                            g.wait_ge(s_se, sbase + k + 1)
                            SPS.issue(g.dma_start(kt_in[128 * 15:128 * 15 + 128, 512 * hh:512 * hh + 512],
                                                  STG[k % 4][:]))

                vxn = C["ve"]

                @blk.tensor
                def _(t_):
                    t_.wait_ge(s_ve, vxn)
                    for e in range(16):
                        LPW.wait(t_, ld0 + e)
                        for h in range(2):
                            if e >= 2:
                                t_.wait_ge(s_se, sbase + 2 * (e - 2) + h + 1)
                            for d in range(16):
                                mi = t_.matmul(PS[2 * h + e % 2][:, :],
                                               WSB[e % 2][:, 128 * d:128 * d + 128],
                                               XN(d, h), start=(d == 0), stop=(d == 15))
                            pe(mi)

                @blk.scalar
                def _(s):
                    for e in range(16):
                        for h in range(2):
                            k = 2 * e + h
                            s.wait_ge(s_pe, peb + k + 1)
                            if dst_is_kt:
                                if k >= 4:
                                    SPS.wait(s, spsb + k - 4)
                                se(s.copy(STG[k % 4][:], PS[2 * h + e % 2][:, :]))
                            else:
                                se(s.copy(QT(e, h), PS[2 * h + e % 2][:, :]))


        def emit_projv2():
            with nc.Block() as blk:
                ld0 = LPW.n
                sbase = C["se"]
                peb = C["pe"]
                spsb = SPS.n

                def st_grp(g, grp):
                    h2, eg2 = grp // 4, grp % 4
                    for sc in range(4):
                        g.wait_ge(s_se, sbase + 4 * grp + sc + 1)
                        r0 = 512 * h2 + 128 * sc
                        SPS.issue(g.dma_start(v_in[r0:r0 + 128, 512 * eg2:512 * eg2 + 512],
                                              STG[sc][:]))

                @blk.gpsimd
                def _(g):
                    for i in range(128):
                        if i % 16 == 0 and i >= 16:
                            st_grp(g, i // 16 - 1)
                        if i >= 2:
                            g.wait_ge(s_pe, peb + i - 1)
                        d = i % 16
                        eg = (i // 16) % 4
                        LPW.issue(g.dma_start(WVB[i % 2][:],
                                               wv[128 * d:128 * d + 128, 512 * eg:512 * eg + 512]))
                    st_grp(g, 7)
                    g.wait_ge(s_ve, 1)
                    for r in range(8):
                        SPS.issue(g.dma_start(v_in[128 * r:128 * r + 128, 2048:2176], C1T[:]))

                vxn2 = C["ve"]

                @blk.tensor
                def _(t_):
                    t_.wait_ge(s_ve, vxn2)
                    for i in range(128):
                        d = i % 16
                        h = i // 64
                        grp = i // 16
                        LPW.wait(t_, ld0 + i)
                        if d == 0 and grp >= 2:
                            t_.wait_ge(s_se, sbase + 4 * (grp - 1))
                        for sc in range(4):
                            mi = t_.matmul(PS[4 + sc][:, :], XNs(d, h, sc), WVB[i % 2][:],
                                           start=(d == 0), stop=(d == 15))
                        pe(mi)

                @blk.scalar
                def _(s):
                    for grp in range(8):
                        s.wait_ge(s_pe, peb + 16 * (grp + 1))
                        if grp >= 1:
                            for sc in range(4):
                                SPS.wait(s, spsb + 4 * (grp - 1) + sc)
                        for sc in range(4):
                            se(s.copy(STG[sc][:], PS[4 + sc][:, :]))


        def emit_gather():
            with nc.Block() as blk:
                @blk.gpsimd
                def _(g):
                    SPS.wait_all(g)
                    grp = [[0, 1, 2, 3], [4, 5, 6, 7]]
                    cc(g.collective_compute("AllGather", ALU.bypass, replica_groups=grp,
                                            ins=[kt_in.ap().opt()], outs=[kt_out.ap().opt()]))
                    cc(g.collective_compute("AllGather", ALU.bypass, replica_groups=grp,
                                            ins=[v_in.ap().opt()], outs=[v_out.ap().opt()]))

        def emit_attn(h):
            ccw = C["cc"]
            with nc.Block() as blk:
                ld0 = LPK.n
                sbase = C["se"]
                peb = C["pe"]

                @blk.sync
                def _(sy):
                    sy.wait_ge(s_cc, ccw)
                    for kc in range(32):
                        if kc >= 2:
                            sy.wait_ge(s_pe, peb + kc - 1)
                        own, off = kslice(kc)
                        LPK.issue(sy.dma_start(
                            KCB[kc % 2].rearrange("p (t e) -> p t e", t=16),
                            kt_out.rearrange("(t p) s -> p t s", p=128)[:, 16 * own:16 * own + 16, off:off + 128]))

                @blk.tensor
                def _(t_):
                    t_.wait_ge(s_se, sbase)
                    for kc in range(32):
                        LPK.wait(t_, ld0 + kc)
                        if kc >= 2:
                            t_.wait_ge(s_se, sbase + kc - 1)
                        for e in range(16):
                            mi = t_.matmul(PS[kc % 2][:, :], KCB[kc % 2][:, 128 * e:128 * e + 128],
                                           QT(e, h), start=(e == 0), stop=(e == 15))
                        pe(mi)

                @blk.scalar
                def _(s):
                    for kc in range(32):
                        s.wait_ge(s_pe, peb + kc + 1)
                        se(s.activation(PT(kc), PS[kc % 2][:, :], AF.Exp,
                                        scale=float(1.0 / np.sqrt(D))))

            with nc.Block() as blk:
                ld1 = LPK.n
                seb = C["se"] - 32

                @blk.sync
                def _(sy):
                    for b in range(8):
                        own = b if b < 4 else 7 - b
                        off = 0 if b < 4 else 512
                        t0 = 8 * own + off // 128
                        LPK.issue(sy.dma_start(
                            VCB[0][:, 512 * b:512 * b + 512].rearrange("p (t e) -> p t e", t=4),
                            v_out.rearrange("(t p) e -> p t e", p=128)[:, t0:t0 + 4, 2048:2176]))

                @blk.tensor
                def _(t_):
                    for b in range(8):
                        LPK.wait(t_, ld1 + b)
                    t_.wait_ge(s_se, seb + 32)
                    for kc in range(32):
                        mi = t_.matmul(PS[6][:, :], VCB[0][:, 128 * kc:128 * kc + 128], PT(kc),
                                       start=(kc == 0), stop=(kc == 31))
                    pe(mi)
                    mi = t_.matmul(PS[7][:, :], C1T[:], PT(0), start=True, stop=True)
                    pe(mi)

                @blk.vector
                def _(v):
                    v.wait_ge(s_pe, C["pe"])
                    ve(v.reciprocal(ROW[0:1, 2048:2560], PS[6][0:1, :]))

                @blk.gpsimd
                def _(g):
                    g.wait_ge(s_ve, C["ve"])
                    SPS.issue(g.dma_start(row_d[0:1, 2048:2560], ROW[0:1, 2048:2560]))

                @blk.sync
                def _(sy):
                    SPS.wait(sy, SPS.n - 1)
                    LPK.issue(sy.dma_start(SLOT(16)[:], bass.AP(row_d, 2048, [[0, 128], [1, 512]])))

            idx_bl = LPK.n - 1
            with nc.Block() as blk:
                ld2 = LPK.n
                vb2 = C["ve"]
                peb3 = C["pe"]

                @blk.sync
                def _(sy):
                    for e in range(16):
                        if e >= 2:
                            sy.wait_ge(s_ve, vb2 + e - 1)
                        for b in range(8):
                            own = b if b < 4 else 7 - b
                            off = 0 if b < 4 else 512
                            t0 = 8 * own + off // 128
                            LPK.issue(sy.dma_start(
                                VCB[e % 2][:, 512 * b:512 * b + 512].rearrange("p (t e2) -> p t e2", t=4),
                                v_out.rearrange("(t p) e2 -> p t e2", p=128)[:, t0:t0 + 4, 128 * e:128 * e + 128]))

                @blk.tensor
                def _(t_):
                    for e in range(16):
                        for b in range(8):
                            LPK.wait(t_, ld2 + 8 * e + b)
                        if e >= 2:
                            t_.wait_ge(s_ve, vb2 + e - 1)
                        for kc in range(32):
                            mi = t_.matmul(PS[4 + e % 2][:, :], VCB[e % 2][:, 128 * kc:128 * kc + 128],
                                           PT(kc), start=(kc == 0), stop=(kc == 31))
                        pe(mi)

                @blk.vector
                def _(v):
                    LPK.wait(v, idx_bl)
                    for e in range(16):
                        v.wait_ge(s_pe, peb3 + e + 1)
                        ve(v.tensor_mul(OT(e), PS[4 + e % 2][:, :], SLOT(16)[:]))

            with nc.Block() as blk:
                ld3 = LPW.n
                ldr = LPR.n
                vb3 = C["ve"]
                peb4 = C["pe"]

                @blk.gpsimd
                def _(g):
                    st_idx = {}
                    for dd in range(16):
                        if dd >= 2:
                            g.wait_ge(s_pe, peb4 + dd - 1)
                        LPW.issue(g.dma_start(
                            WSB[dd % 2].rearrange("p (t e) -> p t e", t=16),
                            wo.rearrange("(t p) e -> p t e", p=128)[:, :, 128 * dd:128 * dd + 128]))
                        if dd >= 4:
                            g.wait_ge(s_ve, vb3 + dd - 3)
                            SPS.issue(g.dma_start(rx_d[128 * (dd - 4):128 * (dd - 4) + 128, 512 * h:512 * h + 512],
                                                  RXT[dd % 4][:]) if False else
                                      g.dma_start(rx_d[128 * (dd - 4):128 * (dd - 4) + 128, 512 * h:512 * h + 512],
                                                  RXT[(dd - 4) % 4][:]))
                            SPS.wait(g, SPS.n - 1)
                        LPR.issue(g.dma_start(RXT[dd % 4][:],
                                              rx_d[128 * dd:128 * dd + 128, 512 * h:512 * h + 512]))
                    for dd in range(12, 16):
                        g.wait_ge(s_ve, vb3 + dd + 1)
                        SPS.issue(g.dma_start(rx_d[128 * dd:128 * dd + 128, 512 * h:512 * h + 512],
                                              RXT[dd % 4][:]))

                @blk.tensor
                def _(t_):
                    t_.wait_ge(s_ve, vb3)
                    for dd in range(16):
                        LPW.wait(t_, ld3 + dd)
                        if dd >= 2:
                            t_.wait_ge(s_ve, vb3 + dd - 1)
                        for e in range(16):
                            mi = t_.matmul(PS[2 + dd % 2][:, :], WSB[dd % 2][:, 128 * e:128 * e + 128],
                                           OT(e), start=(e == 0), stop=(e == 15))
                        pe(mi)

                @blk.vector
                def _(v):
                    for dd in range(16):
                        v.wait_ge(s_pe, peb4 + dd + 1)
                        LPR.wait(v, ldr + dd)
                        ve(v.tensor_add(RXT[dd % 4][:], RXT[dd % 4][:], PS[2 + dd % 2][:, :]))

        def emit_ffn(h):
            cu_slot = (2, 9)[h]
            with nc.Block() as blk:
                ld0 = LPW.n
                vb = C["ve"]
                sb = C["se"]
                peb = C["pe"]

                @blk.gpsimd
                def _(g):
                    for fc in range(64):
                        if fc >= 2:
                            g.wait_ge(s_pe, peb + fc - 1)
                        LPW.issue(g.dma_start(
                            QGB[fc % 2].rearrange("p (t f) -> p t f", t=16),
                            qg.rearrange("(t p) f -> p t f", p=128)[:, :, 128 * fc:128 * fc + 128]))
                        LPW.issue(g.dma_start(
                            QUB[fc % 2].rearrange("p (t f) -> p t f", t=16),
                            qu.rearrange("(t p) f -> p t f", p=128)[:, :, 128 * fc:128 * fc + 128]))

                @blk.tensor
                def _(t_):
                    t_.wait_ge(s_ve, vb)
                    for fc in range(64):
                        LPW.wait(t_, ld0 + 2 * fc)
                        LPW.wait(t_, ld0 + 2 * fc + 1)
                        if fc >= 2:
                            t_.wait_ge(s_ve, vb + 3 * (fc - 2) + 2)
                        for d in range(16):
                            t_.matmul(PS[fc % 2][:, :], QGB[fc % 2][:, 128 * d:128 * d + 128],
                                      XN(d, h), start=(d == 0), stop=(d == 15))
                        for d in range(16):
                            mi = t_.matmul(PS[2 + fc % 2][:, :], QUB[fc % 2][:, 128 * d:128 * d + 128],
                                           XN(d, h), start=(d == 0), stop=(d == 15))
                        pe(mi)

                @blk.vector
                def _(v):
                    for fc in range(64):
                        v.wait_ge(s_pe, peb + fc + 1)
                        ve(v.scalar_tensor_tensor(SLOT(14)[:], PS[fc % 2][:, :], GCOL[:, 0:1],
                                                  SLOT(cu_slot)[:], ALU.mult, ALU.mult))
                        ve(v.scalar_tensor_tensor(SLOT(15)[:], PS[2 + fc % 2][:, :], GCOL[:, 1:2],
                                                  SLOT(cu_slot)[:], ALU.mult, ALU.mult))
                        v.wait_ge(s_se, sb + fc + 1)
                        ve(v.tensor_mul(HT(fc), SLOT(17)[:], SLOT(15)[:]))

                @blk.scalar
                def _(s):
                    for fc in range(64):
                        s.wait_ge(s_ve, vb + 3 * fc + 1)
                        se(s.activation(SLOT(17)[:], SLOT(14)[:], AF.Silu))

            with nc.Block() as blk:
                seb3 = C["se"]
                veb3 = C["ve"]

                @blk.scalar
                def _(s):
                    for fc in range(64):
                        if fc >= 2:
                            s.wait_ge(s_ve, veb3 + fc - 1)
                        se(s.activation(SQT[fc % 2][:], HT(fc), AF.Abs))

                @blk.vector
                def _(v):
                    v.memset(AMB[:, 0:512], 0.0)
                    for fc in range(64):
                        v.wait_ge(s_se, seb3 + fc + 1)
                        ve(v.tensor_max(AMB[:, 0:512], AMB[:, 0:512], SQT[fc % 2][:]))

            emit_tokscale(src_amb_cols=512, u_slots=(3,), sb_slots=(4,), cu_slots=(5,))

            with nc.Block() as blk:
                @blk.vector
                def _(v):
                    LPK.wait_all(v)
                    for fc in range(64):
                        v.tensor_mul(SLOT(14)[:], HT(fc), SLOT(4)[:])
                        v.tensor_scalar_add(SLOT(14)[:], SLOT(14)[:], 0.5)
                        v.tensor_copy(ITL[:], SLOT(14)[:])
                        v.tensor_copy(FCT[:], ITL[:])
                        v.tensor_sub(SLOT(15)[:], FCT[:], SLOT(14)[:])
                        v.tensor_scalar(SLOT(15)[:], SLOT(15)[:], 1e30, 1.0, ALU.mult, ALU.min)
                        v.tensor_scalar_max(SLOT(15)[:], SLOT(15)[:], 0.0)
                        ve(v.tensor_sub(HT(fc), FCT[:], SLOT(15)[:]))

            with nc.Block() as blk:
                ld1 = LPW.n
                ldr = LPR.n
                vb3 = C["ve"]
                peb2 = C["pe"]

                @blk.gpsimd
                def _(g):
                    g.wait_ge(s_ve, C["ve"])
                    for dd in range(16):
                        if dd >= 2:
                            g.wait_ge(s_pe, peb2 + dd - 1)
                        LPW.issue(g.dma_start(
                            QDB[dd % 2].rearrange("p (t f) -> p t f", t=64),
                            qd.rearrange("(t p) f -> p t f", p=128)[:, :, 128 * dd:128 * dd + 128]))
                        if dd >= 4:
                            g.wait_ge(s_ve, vb3 + 2 * (dd - 3))
                            SPS.issue(g.dma_start(rx_d[128 * (dd - 4):128 * (dd - 4) + 128, 512 * h:512 * h + 512],
                                                  RXT[(dd - 4) % 4][:]))
                            SPS.wait(g, SPS.n - 1)
                        LPR.issue(g.dma_start(RXT[dd % 4][:],
                                              rx_d[128 * dd:128 * dd + 128, 512 * h:512 * h + 512]))
                    for dd in range(12, 16):
                        g.wait_ge(s_ve, vb3 + 2 * dd + 2)
                        SPS.issue(g.dma_start(rx_d[128 * dd:128 * dd + 128, 512 * h:512 * h + 512],
                                              RXT[dd % 4][:]))

                @blk.tensor
                def _(t_):
                    t_.wait_ge(s_ve, vb3)
                    for dd in range(16):
                        LPW.wait(t_, ld1 + dd)
                        if dd >= 2:
                            t_.wait_ge(s_ve, vb3 + 2 * (dd - 2) + 1)
                        for fc in range(64):
                            mi = t_.matmul(PS[6 + dd % 2][:, :], QDB[dd % 2][:, 128 * fc:128 * fc + 128],
                                           HT(fc), start=(fc == 0), stop=(fc == 63))
                        pe(mi)

                @blk.vector
                def _(v):
                    for dd in range(16):
                        v.wait_ge(s_pe, peb2 + dd + 1)
                        LPR.wait(v, ldr + dd)
                        ve(v.scalar_tensor_tensor(SLOT(14)[:], PS[6 + dd % 2][:, :], GCOL[:, 2:3],
                                                  SLOT(5)[:], ALU.mult, ALU.mult))
                        ve(v.tensor_add(RXT[dd % 4][:], RXT[dd % 4][:], SLOT(14)[:]))

        emit_ln(False)
        if ph >= 2:
            emit_projq(wq, False)
            emit_projq(wk, True)
            emit_projv2()
            emit_gather()
        if ph >= 3:
            for h in range(2):
                emit_attn(h)
        if ph >= 4:
            emit_ln(True)
        if ph >= 9:
            for h in range(2):
                emit_ffn(h)

        with nc.Block() as blk:
            @blk.gpsimd
            def _(g):
                g.wait_ge(s_ve, C["ve"])
                g.wait_ge(s_se, C["se"])
                g.wait_ge(s_pe, C["pe"])
                SPS.wait_all(g)
                SPS.issue(g.dma_start(yt[:, :], rx_d[:, :]))
                if ph < 9:
                    SPS.issue(g.dma_start(dbg["rx"][:, :], rx_d[:, :]))
                    SPS.issue(g.dma_start(dbg["a1"][:, :], A1[:]))
                    SPS.issue(g.dma_start(dbg["a2"][:, :], A2[:]))
                    SPS.issue(g.dma_start(dbg["a3"][:, :], A3[:]))
                SPS.wait_all(g)

    return nc
L1_PHASE = os.environ.get("L1_PHASE", "all")
_PH1 = {"ln3": 1, "proj1": 2, "fcum": 3, "attn1": 4, "ln4": 5, "all": 9}


def build_l1(core):
    ph = _PH1[L1_PHASE]
    j = core % 4
    blocks = core_blocks(core)
    nc = bass.Bass("TRN2", target_bir_lowering=False, detect_race_conditions=False)
    xt = nc.dram_tensor("xt", [D, SL], F32, kind="ExternalInput")
    wq = nc.dram_tensor("wq1t", [D, D], F32, kind="ExternalInput")
    wk = nc.dram_tensor("wk1t", [D, D], F32, kind="ExternalInput")
    wv = nc.dram_tensor("wv1t", [D, D], F32, kind="ExternalInput")
    wo = nc.dram_tensor("wo1t", [D, D], F32, kind="ExternalInput")
    wf = nc.dram_tensor("wf1t", [D, 32], F32, kind="ExternalInput")
    bfb = nc.dram_tensor("bf1", [32, 1], F32, kind="ExternalInput")
    qg = nc.dram_tensor("qw1", [D, FF], F32, kind="ExternalInput")
    qd = nc.dram_tensor("qw2", [FF, D], F32, kind="ExternalInput")
    gam = nc.dram_tensor("gam", [1, 8], F32, kind="ExternalInput")
    id128 = nc.dram_tensor("id128", [128, 128], F32, kind="ExternalInput")
    mskin = nc.dram_tensor("msk", [128, 2048], F32, kind="ExternalInput")
    yt = nc.dram_tensor("yt", [D, SL], F32, kind="ExternalOutput")

    rx_d = nc.dram_tensor("rx_d", [D, SL], F32)
    kt_in = nc.dram_tensor("kt_in", [D, SL], BF16)
    kt_out = nc.dram_tensor("kt_out", [4 * D, SL], BF16)
    q_in = nc.dram_tensor("q_in", [D, SL], BF16)
    q_out = nc.dram_tensor("q_out", [4 * D, SL], BF16)
    v_in = nc.dram_tensor("v_in", [SL, 2080], BF16)
    v_out = nc.dram_tensor("v_out", [S, 2080], BF16)
    f_in = nc.dram_tensor("f_in", [32, SL], F32)
    f_out = nc.dram_tensor("f_out", [128, SL], F32)
    o_in = nc.dram_tensor("o_in", [512, S], BF16)
    o_out = nc.dram_tensor("o_out", [D, S], BF16)
    fcum_d = nc.dram_tensor("fcum_d", [8, S], F32)
    fq_d = nc.dram_tensor("fq_d", [3, 8, S], BF16)
    row_d = nc.dram_tensor("row_d", [1, 4096], F32)
    smax_d = nc.dram_tensor("smax_d", [1, 1024], F32)
    am_d = nc.dram_tensor("am_d", [128, 1024], BF16)

    dbg = {}
    if ph < 9:
        dbg["a1"] = nc.dram_tensor("dbg_a1", [128, 16384], F32, kind="ExternalOutput")
        dbg["a2"] = nc.dram_tensor("dbg_a2", [128, 16384], F32, kind="ExternalOutput")
        dbg["a3"] = nc.dram_tensor("dbg_a3", [128, 32768], F32, kind="ExternalOutput")
        dbg["rx"] = nc.dram_tensor("dbg_rx", [D, SL], F32, kind="ExternalOutput")
        dbg["fc"] = nc.dram_tensor("dbg_fc", [32, S], F32, kind="ExternalOutput")

    with ExitStack() as stk:
        A1 = stk.enter_context(nc.sbuf_tensor("A1", [128, 16384], BF16))
        A2 = stk.enter_context(nc.sbuf_tensor("A2", [128, 16384], BF16))
        A3 = stk.enter_context(nc.sbuf_tensor("A3", [128, 32768], BF16))
        SCR = stk.enter_context(nc.sbuf_tensor("SCR", [128, 7680], F32))
        RXT = [stk.enter_context(nc.sbuf_tensor(f"RXT{i}", [128, 512], F32)) for i in range(4)]
        STG = [stk.enter_context(nc.sbuf_tensor(f"STG{i}", [128, 512], BF16)) for i in range(4)]
        VSTG = [stk.enter_context(nc.sbuf_tensor(f"VSTG{i}", [128, 520], BF16)) for i in range(4)]
        SQT = [stk.enter_context(nc.sbuf_tensor(f"SQT{i}", [128, 512], BF16)) for i in range(2)]
        ITL = stk.enter_context(nc.sbuf_tensor("ITL", [128, 512], I32))
        ROW = stk.enter_context(nc.sbuf_tensor("ROW", [1, 2048], F32))
        VCOL = stk.enter_context(nc.sbuf_tensor("VCOL", [128, 8], F32))
        ONESC = stk.enter_context(nc.sbuf_tensor("ONESC", [128, 1], BF16))
        ONESF = stk.enter_context(nc.sbuf_tensor("ONESF", [128, 1], F32))
        EPSC = stk.enter_context(nc.sbuf_tensor("EPSC", [128, 1], F32))
        ONE1 = stk.enter_context(nc.sbuf_tensor("ONE1", [128, 1], F32))
        GCOL = stk.enter_context(nc.sbuf_tensor("GCOL", [128, 8], F32))
        IDT = stk.enter_context(nc.sbuf_tensor("IDT", [128, 128], F32))
        BFC = stk.enter_context(nc.sbuf_tensor("BFC", [32, 1], F32))
        AMB = stk.enter_context(nc.sbuf_tensor("AMB", [128, 1024], BF16))
        AMT = stk.enter_context(nc.sbuf_tensor("AMT", [128, 1024], BF16))
        MSK = stk.enter_context(nc.sbuf_tensor("MSK", [128, 2048], BF16))
        FTN = stk.enter_context(nc.sbuf_tensor("FTN", [128, 1024], F32))
        PS = [stk.enter_context(nc.psum_tensor(f"PS{i}", [128, 512], F32)) for i in range(8)]
        A3F = A3.bitcast(F32)
        FCA = A3F[0:32, 0:1024]
        FCB = A3F[0:32, 1024:2048]
        FLO = A3F[0:32, 2048:3072]
        FSP = A3[0:32, 6144:7168]

        s_pe = stk.enter_context(nc.semaphore("s_pe"))
        s_ve = stk.enter_context(nc.semaphore("s_ve"))
        s_se = stk.enter_context(nc.semaphore("s_se"))
        s_cc = stk.enter_context(nc.semaphore("s_cc"))
        LPW = _St(nc, stk, "lpw")
        LPK = _St(nc, stk, "lpk")
        LPR = _St(nc, stk, "lpr")
        SPS = _St(nc, stk, "sps")

        C = {"pe": 0, "ve": 0, "se": 0, "cc": 0}

        def pe(mi):
            mi.then_inc(s_pe, 1); C["pe"] += 1; return C["pe"]

        def ve(mi):
            mi.then_inc(s_ve, 1); C["ve"] += 1; return C["ve"]

        def se(mi):
            mi.then_inc(s_se, 1); C["se"] += 1; return C["se"]

        def cc(mi):
            mi.then_inc(s_cc, 1); C["cc"] += 1; return C["cc"]

        def XN(t, h):
            return A1[:, 1024 * t + 512 * h: 1024 * t + 512 * h + 512]

        def OT(e):
            return A1[:, 512 * e:512 * e + 512]

        def HT(fc):
            return A3[:, 512 * fc:512 * fc + 512]

        def SLOT(i):
            return SCR[:, 512 * i:512 * i + 512]
        KTH = [A3[0:67, 0:4096], A3[0:67, 4096:8192]]
        QTA = [A3[0:67, 8192:12288], A3[0:67, 12288:16384]]
        VPB = [A3[:, 16384:20544], A3[:, 20544:24704]]
        PTB = [A3[:, 24704:25728], A3[:, 25728:26752]]
        QGB = [A2[:, 0:2048], A2[:, 2048:4096]]
        QDB = [A2[:, 0:8192], A2[:, 8192:16384]]

        with nc.Block() as blk:
            @blk.sync
            def _(sy):
                LPK.issue(sy.dma_start(GCOL[:], bass.AP(gam, 0, [[0, 128], [1, 8]])))
                LPK.issue(sy.dma_start(IDT[:], id128[:, :]))
                LPK.issue(sy.dma_start(BFC[:], bfb[:, :]))

            @blk.gpsimd
            def _(g):
                LPW.issue(g.dma_start(MSK[:], mskin[:, :]))
                SPS.issue(g.dma_start(rx_d[:, :], xt[:, :]))

            @blk.vector
            def _(v):
                v.memset(ONESC[:], 1.0)
                v.memset(ONESF[:], 1.0)
                v.memset(EPSC[:], EPS_LN)
                v.memset(ONE1[:], 1.0)
                for i in range(4):
                    v.memset(VSTG[i][:], 0.0)
                    ve(v.memset(bass.AP(VSTG[i], 64, [[520, 128], [65, 8]]), 1.0))

        def emit_ln(quant):
            rx_ready = SPS.n - 1
            with nc.Block() as blk:
                ld0 = LPR.n
                sbase = C["se"]
                pebase = C["pe"]

                @blk.gpsimd
                def _(g):
                    SPS.wait(g, rx_ready)
                    SPS.wait(g, rx_ready - 1)
                    for i in range(32):
                        if i >= 4:
                            g.wait_ge(s_pe, pebase + i - 3)
                            g.wait_ge(s_se, sbase + i - 3)
                        t, h = i % 16, i // 16
                        LPR.issue(g.dma_start(RXT[i % 4][:],
                                              rx_d[128 * t:128 * t + 128, 512 * h:512 * h + 512]))

                @blk.scalar
                def _(s):
                    for i in range(32):
                        LPR.wait(s, ld0 + i)
                        if i >= 2:
                            s.wait_ge(s_pe, pebase + i - 1)
                        se(s.activation(SQT[i % 2][:], RXT[i % 4][:], AF.Square))

                @blk.tensor
                def _(t_):
                    for i in range(32):
                        t, h = i % 16, i // 16
                        LPR.wait(t_, ld0 + i)
                        t_.wait_ge(s_se, sbase + i + 1)
                        t_.matmul(PS[2 * h][0:1, :], ONESF[:], RXT[i % 4][:],
                                  start=(t == 0), stop=(t == 15))
                        mi = t_.matmul(PS[2 * h + 1][0:1, :], ONESC[:], SQT[i % 2][:],
                                       start=(t == 0), stop=(t == 15))
                        pe(mi)
                    mi = t_.matmul(PS[4][0:1, :], ONESC[:], STG[1][:], start=True, stop=True)
                    pe(mi)

            with nc.Block() as blk:
                @blk.vector
                def _(v):
                    v.wait_ge(s_pe, C["pe"])
                    v.tensor_copy(ROW[0:1, 0:512], PS[0][0:1, :])
                    v.tensor_copy(ROW[0:1, 512:1024], PS[1][0:1, :])
                    v.tensor_copy(ROW[0:1, 1024:1536], PS[2][0:1, :])
                    ve(v.tensor_copy(ROW[0:1, 1536:2048], PS[3][0:1, :]))

                @blk.sync
                def _(sy):
                    sy.wait_ge(s_ve, C["ve"])
                    i0 = LPK.issue(sy.dma_start(row_d[0:1, 0:2048], ROW[0:1, 0:2048]))
                    LPK.wait(sy, i0)
                    for h in range(2):
                        LPK.issue(sy.dma_start(SLOT(0 + 7 * h)[:],
                                               bass.AP(row_d, 1024 * h, [[0, 128], [1, 512]])))
                        LPK.issue(sy.dma_start(SLOT(1 + 7 * h)[:],
                                               bass.AP(row_d, 1024 * h + 512, [[0, 128], [1, 512]])))

            with nc.Block() as blk:
                @blk.vector
                def _(v):
                    LPK.wait_all(v)
                    for h in range(2):
                        v.tensor_scalar_mul(SLOT(2 + 7 * h)[:], SLOT(0 + 7 * h)[:], 1.0 / D)
                        v.tensor_scalar_mul(SLOT(3 + 7 * h)[:], SLOT(1 + 7 * h)[:], 1.0 / D)
                    for h in range(2):
                        v.tensor_mul(SLOT(0 + 7 * h)[:], SLOT(2 + 7 * h)[:], SLOT(2 + 7 * h)[:])
                    for h in range(2):
                        ve(v.tensor_sub(SLOT(4 + 7 * h)[:], SLOT(3 + 7 * h)[:], SLOT(0 + 7 * h)[:]))

                @blk.scalar
                def _(s):
                    s.wait_ge(s_ve, C["ve"])
                    for h in range(2):
                        se(s.activation(SLOT(3 + 7 * h)[:], SLOT(4 + 7 * h)[:], AF.Sqrt, bias=EPSC[:]))

                @blk.vector
                def _(v):
                    v.wait_ge(s_se, C["se"])
                    for h in range(2):
                        v.tensor_scalar_add(SLOT(4 + 7 * h)[:], SLOT(4 + 7 * h)[:], EPS_LN)
                    for h in range(2):
                        v.reciprocal(SLOT(5 + 7 * h)[:], SLOT(3 + 7 * h)[:])
                    for step in range(2):
                        for h in range(2):
                            v.tensor_mul(SLOT(14)[:] if h == 0 else SLOT(1)[:],
                                         SLOT(5 + 7 * h)[:], SLOT(5 + 7 * h)[:])
                        for h in range(2):
                            t = SLOT(14) if h == 0 else SLOT(1)
                            v.tensor_mul(t[:], t[:], SLOT(4 + 7 * h)[:])
                        for h in range(2):
                            t = SLOT(14) if h == 0 else SLOT(1)
                            v.tensor_scalar(t[:], t[:], -0.5, 1.5, ALU.mult, ALU.add)
                        for h in range(2):
                            t = SLOT(14) if h == 0 else SLOT(1)
                            v.tensor_mul(SLOT(5 + 7 * h)[:], SLOT(5 + 7 * h)[:], t[:])
                    for h in range(2):
                        ve(v.tensor_mul(SLOT(6 + 7 * h)[:], SLOT(2 + 7 * h)[:], SLOT(5 + 7 * h)[:]))

            with nc.Block() as blk:
                ld1 = LPR.n
                vbase2 = C["ve"]
                kv = 2 if quant else 1
                sebase2 = C["se"]

                @blk.gpsimd
                def _(g):
                    for i in range(32):
                        if i >= 4:
                            g.wait_ge(s_ve, vbase2 + kv * (i - 3))
                        t, h = i % 16, i // 16
                        LPR.issue(g.dma_start(RXT[i % 4][:],
                                              rx_d[128 * t:128 * t + 128, 512 * h:512 * h + 512]))

                @blk.vector
                def _(v):
                    if quant:
                        v.memset(AMB[:], 0.0)
                    for i in range(32):
                        t, h = i % 16, i // 16
                        LPR.wait(v, ld1 + i)
                        v.tensor_mul(SLOT(14)[:], RXT[i % 4][:], SLOT(5 + 7 * h)[:])
                        ve(v.tensor_sub(XN(t, h), SLOT(14)[:], SLOT(6 + 7 * h)[:]))
                        if quant:
                            v.wait_ge(s_se, sebase2 + i + 1)
                            ve(v.tensor_max(AMB[:, 512 * h:512 * h + 512],
                                            AMB[:, 512 * h:512 * h + 512], SQT[i % 2][:]))

                if quant:
                    @blk.scalar
                    def _(s):
                        for i in range(32):
                            t, h = i % 16, i // 16
                            s.wait_ge(s_ve, vbase2 + 2 * i + 1)
                            if i >= 2:
                                s.wait_ge(s_ve, vbase2 + 2 * i - 2)
                            se(s.activation(SQT[i % 2][:], XN(t, h), AF.Abs))

            if not quant:
                return
            emit_tokscale(1024, (0, 7), (1, 8), (2, 9))
            with nc.Block() as blk:
                @blk.vector
                def _(v):
                    LPK.wait_all(v)
                    for i in range(32):
                        t, h = i % 16, i // 16
                        v.tensor_mul(SLOT(14)[:], XN(t, h), SLOT((1, 8)[h])[:])
                        v.tensor_scalar_add(SLOT(14)[:], SLOT(14)[:], 0.5)
                        v.tensor_copy(ITL[:], SLOT(14)[:])
                        v.tensor_copy(SLOT(13)[:], ITL[:])
                        v.tensor_sub(SLOT(7)[:], SLOT(13)[:], SLOT(14)[:])
                        v.tensor_scalar(SLOT(7)[:], SLOT(7)[:], 1e30, 1.0, ALU.mult, ALU.min)
                        v.tensor_scalar_max(SLOT(7)[:], SLOT(7)[:], 0.0)
                        ve(v.tensor_sub(XN(t, h), SLOT(13)[:], SLOT(7)[:]))

        def emit_tokscale(src_amb_cols, u_slots, sb_slots, cu_slots):
            nb = src_amb_cols // 128
            with nc.Block() as blk:
                @blk.gpsimd
                def _(g):
                    g.wait_ge(s_ve, C["ve"])
                    SPS.issue(g.dma_start(am_d[:, 0:src_amb_cols], AMB[:, 0:src_amb_cols]))

                @blk.sync
                def _(sy):
                    SPS.wait(sy, SPS.n - 1)
                    for b in range(nb):
                        LPK.issue(sy.dma_start(AMT[:, 128 * b:128 * b + 128],
                                               am_d[:, 128 * b:128 * b + 128], transpose=True))

            with nc.Block() as blk:
                ldb = LPK.n - nb

                @blk.vector
                def _(v):
                    for b in range(nb):
                        LPK.wait(v, ldb + b)
                        ve(v.reduce_max(VCOL[:, b:b + 1], AMT[:, 128 * b:128 * b + 128], axis=AX.X))

                @blk.gpsimd
                def _(g):
                    g.wait_ge(s_ve, C["ve"])
                    for b in range(nb):
                        SPS.issue(g.dma_start(smax_d[0:1, 128 * b:128 * b + 128], VCOL[:, b:b + 1]))

                @blk.sync
                def _(sy):
                    for b in range(nb):
                        SPS.wait(sy, SPS.n - nb + b)
                    for h in range(nb // 4):
                        LPK.issue(sy.dma_start(SLOT(u_slots[h])[:],
                                               bass.AP(smax_d, 512 * h, [[0, 128], [1, 512]])))

            with nc.Block() as blk:
                @blk.vector
                def _(v):
                    LPK.wait_all(v)
                    hs = range(nb // 4)
                    for h in hs:
                        v.tensor_scalar_max(SLOT(u_slots[h])[:], SLOT(u_slots[h])[:], EPS_Q)
                    for h in hs:
                        v.reciprocal(SLOT(sb_slots[h])[:], SLOT(u_slots[h])[:])
                    for h in hs:
                        v.tensor_scalar_mul(SLOT(sb_slots[h])[:], SLOT(sb_slots[h])[:], 127.0)
                    for h in hs:
                        ve(v.tensor_scalar_mul(SLOT(cu_slots[h])[:], SLOT(u_slots[h])[:], 1.0 / 127.0))

        def emit_projq(wsrc, dst, scale=None):
            with nc.Block() as blk:
                ld0 = LPW.n
                sbase = C["se"]
                peb = C["pe"]
                spsb = SPS.n
                vxn = C["ve"]

                @blk.gpsimd
                def _(g):
                    for e in range(16):
                        if e >= 1:
                            for hh in range(2):
                                k = 2 * (e - 1) + hh
                                g.wait_ge(s_se, sbase + k + 1)
                                SPS.issue(g.dma_start(dst[128 * (e - 1):128 * (e - 1) + 128,
                                                          512 * hh:512 * hh + 512],
                                                      STG[k % 4][:]))
                        if e >= 2:
                            g.wait_ge(s_se, sbase + 2 * (e - 1))
                        LPW.issue(g.dma_start(
                            WSB[e % 2].rearrange("p (t e) -> p t e", t=16),
                            wsrc.rearrange("(t p) e -> p t e", p=128)[:, :, 128 * e:128 * e + 128]))
                    for hh in range(2):
                        k = 30 + hh
                        g.wait_ge(s_se, sbase + k + 1)
                        SPS.issue(g.dma_start(dst[128 * 15:128 * 15 + 128, 512 * hh:512 * hh + 512],
                                              STG[k % 4][:]))

                @blk.tensor
                def _(t_):
                    t_.wait_ge(s_ve, vxn)
                    t_.wait_ge(s_se, sbase)
                    for e in range(16):
                        LPW.wait(t_, ld0 + e)
                        for h in range(2):
                            if e >= 2:
                                t_.wait_ge(s_se, sbase + 2 * (e - 2) + h + 1)
                            for d in range(16):
                                mi = t_.matmul(PS[2 * h + e % 2][:, :],
                                               WSB[e % 2][:, 128 * d:128 * d + 128],
                                               XN(d, h), start=(d == 0), stop=(d == 15))
                            pe(mi)

                @blk.scalar
                def _(s):
                    for e in range(16):
                        for h in range(2):
                            k = 2 * e + h
                            s.wait_ge(s_pe, peb + k + 1)
                            if k >= 4:
                                SPS.wait(s, spsb + k - 4)
                            if scale is not None:
                                se(s.activation(STG[k % 4][:], PS[2 * h + e % 2][:, :], AF.Copy,
                                                scale=scale))
                            else:
                                se(s.copy(STG[k % 4][:], PS[2 * h + e % 2][:, :]))

        WSB = [A3[:, 0:2048], A3[:, 2048:4096]]
        WVB = [A3[:, 4096:4608], A3[:, 4608:5120]]

        def emit_projv():
            with nc.Block() as blk:
                ld0 = LPW.n
                sbase = C["se"]
                peb = C["pe"]
                spsb = SPS.n
                vxn2 = C["ve"]

                def st_grp(g, grp):
                    h2, eg2 = grp // 4, grp % 4
                    for sc in range(4):
                        g.wait_ge(s_se, sbase + 4 * grp + sc + 1)
                        r0 = 512 * h2 + 128 * sc
                        SPS.issue(g.dma_start(v_in[r0:r0 + 128, 520 * eg2:520 * eg2 + 520],
                                              VSTG[sc][:]))

                @blk.gpsimd
                def _(g):
                    for i in range(128):
                        if i % 16 == 0 and i >= 16:
                            st_grp(g, i // 16 - 1)
                        if i >= 2:
                            g.wait_ge(s_pe, peb + i - 1)
                        d = i % 16
                        eg = (i // 16) % 4
                        LPW.issue(g.dma_start(WVB[i % 2][:],
                                              wv[128 * d:128 * d + 128, 512 * eg:512 * eg + 512]))
                    st_grp(g, 7)

                @blk.tensor
                def _(t_):
                    t_.wait_ge(s_ve, vxn2)
                    for i in range(128):
                        d = i % 16
                        h = i // 64
                        grp = i // 16
                        LPW.wait(t_, ld0 + i)
                        if d == 0 and grp >= 2:
                            t_.wait_ge(s_se, sbase + 4 * (grp - 1))
                        for sc in range(4):
                            mi = t_.matmul(PS[4 + sc][:, :],
                                           A1[:, 1024 * d + 512 * h + 128 * sc:
                                              1024 * d + 512 * h + 128 * sc + 128],
                                           WVB[i % 2][:], start=(d == 0), stop=(d == 15))
                        pe(mi)

                @blk.scalar
                def _(s):
                    for grp in range(8):
                        s.wait_ge(s_pe, peb + 16 * (grp + 1))
                        if grp >= 1:
                            for sc in range(4):
                                SPS.wait(s, spsb + 4 * (grp - 1) + sc)
                        for sc in range(4):
                            se(s.copy(bass.AP(VSTG[sc], 0, [[520, 128], [65, 8], [1, 64]]),
                                      PS[4 + sc][:, :].rearrange("p (a b) -> p a b", a=8)))

        WFB = A3[:, 5120:5632]
        lrow_d = nc.dram_tensor("lrow_d", [2, 4096], F32)
        kt_my = nc.dram_tensor("kt_my", [2048, SL], BF16)
        q_my = nc.dram_tensor("q_my", [2048, SL], BF16)
        v_my = nc.dram_tensor("v_my", [S, 520], BF16)
        f_my = nc.dram_tensor("f_my", [32, SL], F32)
        o_my = nc.dram_tensor("o_my", [D, SL], BF16)
        if ph < 9:
            dbg["q"] = nc.dram_tensor("dbg_q", [D, SL], F32, kind="ExternalOutput")
            dbg["k"] = nc.dram_tensor("dbg_k", [D, SL], F32, kind="ExternalOutput")
            dbg["v"] = nc.dram_tensor("dbg_v", [SL, 2080], F32, kind="ExternalOutput")
            dbg["f"] = nc.dram_tensor("dbg_f", [32, SL], F32, kind="ExternalOutput")
            dbg["o"] = nc.dram_tensor("dbg_o", [D, S], F32, kind="ExternalOutput")
            dbg["ftn"] = nc.dram_tensor("dbg_ftn", [128, 1024], F32, kind="ExternalOutput")

        A2F = A2.bitcast(F32)
        OSTG_F = [A2F[0:64, 0:4096], A2F[0:64, 4096:8192]]
        LRR = [A2F[64:65, 0:4096], A2F[64:65, 4096:8192]]
        FH = A2F[0:8, 0:4096]
        STA = A1[0:8, 0:4096]
        TMPF = SCR[0:8, 0:4096]

        def emit_projf():
            with nc.Block() as blk:
                ld0 = LPW.n
                vxn3 = C["ve"]
                se_gate = C["se"]

                @blk.gpsimd
                def _(g):
                    LPW.issue(g.dma_start(WFB.rearrange("p (t f) -> p t f", t=16),
                                          wf.rearrange("(t p) f -> p t f", p=128)))

                @blk.tensor
                def _(t_):
                    t_.wait_ge(s_ve, vxn3)
                    t_.wait_ge(s_se, se_gate)
                    LPW.wait(t_, ld0)
                    for h in range(2):
                        for d in range(16):
                            mi = t_.matmul(PS[h][0:32, :], WFB[:, 32 * d:32 * d + 32], XN(d, h),
                                           start=(d == 0), stop=(d == 15))
                        pe(mi)
                    mi = t_.matmul(PS[2][0:1, :], ONESC[:], STG[0][:], start=True, stop=True)
                    pe(mi)

                @blk.vector
                def _(v):
                    v.wait_ge(s_pe, C["pe"])
                    for h in range(2):
                        ve(v.tensor_scalar(FCA[0:32, 512 * h:512 * h + 512], PS[h][0:32, :],
                                           BFC[:], None, ALU.add))

                @blk.scalar
                def _(s):
                    s.wait_ge(s_pe, C["pe"])
                    for h in range(2):
                        s.activation(FCB[:32, 512 * h:512 * h + 512], PS[h][0:32, :], AF.Abs, bias=BFC[:])
                    for h in range(2):
                        s.activation(FCB[:32, 512 * h:512 * h + 512], FCB[:32, 512 * h:512 * h + 512],
                                     AF.Exp, scale=-1.0)
                    for h in range(2):
                        se(s.activation(FCB[:32, 512 * h:512 * h + 512], FCB[:32, 512 * h:512 * h + 512],
                                        AF.Ln, bias=ONE1[0:32, :]))

                @blk.vector
                def _(v):
                    v.wait_ge(s_se, C["se"])
                    v.tensor_scalar_min(FCA, FCA, 0.0)
                    ve(v.tensor_sub(FLO, FCA, FCB))

            with nc.Block() as blk:
                @blk.gpsimd
                def _(g):
                    g.wait_ge(s_ve, C["ve"])
                    SPS.issue(g.dma_start(f_in[:, :], FLO))

        def emit_gather():
            with nc.Block() as blk:
                @blk.gpsimd
                def _(g):
                    SPS.wait_all(g)
                    grp = [[0, 1, 2, 3], [4, 5, 6, 7]]
                    for t_in, t_out in ((f_in, f_out), (kt_in, kt_out),
                                        (q_in, q_out), (v_in, v_out)):
                        cc(g.collective_compute("AllGather", ALU.bypass, replica_groups=grp,
                                                ins=[t_in.ap().opt()], outs=[t_out.ap().opt()]))

        def emit_fcopy(cc_f):
            idx = [0]
            with nc.Block() as blk:
                @blk.gpsimd
                def _(g):
                    g.wait_ge(s_cc, cc_f)
                    pid = nc.partition_id(engines=[mybir.EngineType.Pool])
                    jj = pid % 4
                    idx[0] = SPS.issue(g.dma_start(
                        f_my.rearrange("(s h) t -> s h t", h=8),
                        bass.AP(f_out, jj * (8 * SL), [[32 * SL, 4], [SL, 8], [1, SL]])))
            return idx[0]

        def emit_kqv(cc0):
            idxs = {}
            with nc.Block() as blk:
                @blk.gpsimd
                def _(g):
                    pid = nc.partition_id(engines=[mybir.EngineType.Pool])
                    jj = pid % 4
                    g.wait_ge(s_cc, cc0 + 2)
                    idxs["kt"] = SPS.issue(g.dma_start(
                        kt_my.rearrange("(r d) t -> r d t", d=512),
                        bass.AP(kt_out, jj * (512 * SL), [[2048 * SL, 4], [SL, 512], [1, SL]])))
                    g.wait_ge(s_cc, cc0 + 3)
                    idxs["q"] = SPS.issue(g.dma_start(
                        q_my.rearrange("(r d) t -> r d t", d=512),
                        bass.AP(q_out, jj * (512 * SL), [[2048 * SL, 4], [SL, 512], [1, SL]])))
                    g.wait_ge(s_cc, cc0 + 4)
                    idxs["v"] = SPS.issue(g.dma_start(
                        v_my[:, :], bass.AP(v_out, jj * 520, [[2080, S], [1, 520]])))
            return idxs

        def emit_fcum(f_idx):
            seg_copy_ve = {}
            seg_store = {}
            for seg in range(4):
                with nc.Block() as blk:
                    lk0 = LPK.n
                    ve_base = C["ve"]

                    @blk.gpsimd
                    def _(g):
                        g.wait_ge(s_ve, ve_base + 1)
                        seg_store[seg] = SPS.issue(
                            g.dma_start(fcum_d[:, 1024 * seg:1024 * seg + 1024],
                                        FLO[0:8, 0:1024]))

                    @blk.sync
                    def _(sy):
                        if seg == 0:
                            SPS.wait(sy, f_idx)
                        else:
                            g_ve = seg_copy_ve[seg - 1]
                            sy.wait_ge(s_ve, g_ve)
                        LPK.issue(sy.dma_start(FCA[0:8, 0:1024],
                                               f_my[8 * seg:8 * seg + 8, :]))
                        if seg > 0:
                            SPS.wait(sy, seg_store[seg - 1])
                            LPK.issue(sy.dma_start(VCOL[0:8, 0:1],
                                                   fcum_d[:, 1024 * seg - 1:1024 * seg],
                                                   allow_slow_non_contiguous=True))

                    @blk.vector
                    def _(v):
                        LPK.wait(v, lk0)
                        if seg > 0:
                            LPK.wait(v, LPK.n - 1)
                        src, dst = FCA, FCB
                        for k in range(10):
                            sh = 1 << k
                            v.tensor_copy(dst[0:8, 0:sh], src[0:8, 0:sh])
                            v.tensor_add(dst[0:8, sh:1024], src[0:8, sh:1024],
                                         src[0:8, 0:1024 - sh])
                            src, dst = dst, src
                        if seg > 0:
                            v.tensor_scalar(src[0:8, 0:1024], src[0:8, 0:1024],
                                            VCOL[0:8, 0:1], None, ALU.add)
                        seg_copy_ve[seg] = ve(v.tensor_copy(FLO[0:8, 0:1024], src[0:8, 0:1024]))

            with nc.Block() as blk:
                ld1 = LPK.n
                peb = C["pe"]
                vb = C["ve"]
                sl = SPS.n - 1

                @blk.sync
                def _(sy):
                    SPS.wait(sy, sl)
                    SPS.wait(sy, sl - 1)
                    for seg in range(4):
                        if seg >= 2:
                            sy.wait_ge(s_pe, peb + 8 * (seg - 1))
                        LPK.issue(sy.dma_start((FCA if seg % 2 == 0 else FCB)[0:8, 0:1024],
                                               fcum_d[:, 1024 * seg:1024 * seg + 1024]))

                @blk.tensor
                def _(t_):
                    for seg in range(4):
                        LPK.wait(t_, ld1 + seg)
                        buf = FCA if seg % 2 == 0 else FCB
                        for b8 in range(8):
                            i = 8 * seg + b8
                            if i >= 4:
                                t_.wait_ge(s_ve, vb + i - 3)
                            mi = t_.transpose(PS[i % 4][0:128, 0:8],
                                              buf[0:8, 128 * b8:128 * b8 + 128], IDT[0:8, 0:8])
                            pe(mi)
                    mi = t_.matmul(PS[4][0:1, :], ONESC[:], STG[0][:], start=True, stop=True)
                    pe(mi)

                @blk.vector
                def _(v):
                    for i in range(32):
                        v.wait_ge(s_pe, min(peb + i + 2, peb + 33))
                        ve(v.tensor_scalar_mul(FTN[:, 8 * i:8 * i + 8],
                                               PS[i % 4][0:128, 0:8], -1.0))

            with nc.Block() as blk:
                ld2 = LPK.n
                veb = C["ve"]
                pe_cap = C["pe"]
                fq_sps = []

                @blk.sync
                def _(sy):
                    LPK.issue(sy.dma_start(FH, fcum_d[:, :]))

                @blk.gpsimd
                def _(g):
                    for i in range(3):
                        g.wait_ge(s_ve, veb + i + 1)
                        fq_sps.append(SPS.issue(
                            g.dma_start(bass.AP(fq_d, i * 8 * S, [[S, 8], [1, S]]), STA)))

                @blk.vector
                def _(v):
                    LPK.wait(v, ld2)
                    v.wait_ge(s_pe, pe_cap)
                    cnt = ve(v.tensor_copy(STA, FH))
                    assert cnt == veb + 1
                    v.tensor_copy(TMPF, STA)
                    v.tensor_sub(FH, FH, TMPF)
                    SPS.wait(v, fq_sps[0])
                    ve(v.tensor_copy(STA, FH))
                    v.tensor_copy(TMPF, STA)
                    v.tensor_sub(FH, FH, TMPF)
                    SPS.wait(v, fq_sps[1])
                    ve(v.tensor_copy(STA, FH))
            return fq_sps[-1]

        def emit_attn1(kqv, fq_sps):
            pe_fcum = C["pe"]
            bank_exp = {0: 0, 1: 0}
            bank_pv = {0: 0, 1: 0}
            bank_ot = {0: 0, 1: 0}
            head_pe_done = {}
            head_lpr = {}
            vpb_lpr = {}
            n_lpk_bcast = {}
            n_store_idx = {}

            with nc.Block() as blk:
                @blk.vector
                def _(v):
                    v.wait_ge(s_pe, pe_fcum)
                    LPW.wait(v, 0)
                    for m in range(4):
                        v.tensor_copy(SLOT(m)[:], MSK[:, 512 * m:512 * m + 512])
                    v.memset(KTH[0][64:67, :], 1.0)
                    ve(v.memset(KTH[1][64:67, :], 1.0))
            ve_ones = C["ve"]

            def emit_norm(hl):
                hb = hl % 2
                ve_base = C["ve"]
                lpk0 = LPK.n
                sps0 = SPS.n
                stores = []
                with nc.Block() as blk:
                    @blk.gpsimd
                    def _(g):
                        g.wait_ge(s_ve, ve_base)
                        if hl >= 2:
                            LPK.wait(g, n_lpk_bcast[hl - 2])
                        SPS.issue(g.dma_start(lrow_d[hb:hb + 1, :], LRR[hb][0:1, :]))
                        for qb in range(8):
                            g.wait_ge(s_ve, ve_base + qb + 1)
                            stores.append(SPS.issue(
                                g.dma_start(o_in[64 * hl:64 * hl + 64, 512 * qb:512 * qb + 512],
                                            STG[2 + qb % 2][0:64, :])))

                    @blk.sync
                    def _(sy):
                        SPS.wait(sy, sps0)
                        for qb in range(8):
                            if qb >= 2:
                                sy.wait_ge(s_ve, ve_base + qb - 1)
                            LPK.issue(sy.dma_start(SLOT(10 + qb % 2)[:],
                                                   bass.AP(lrow_d, 4096 * hb + 512 * qb,
                                                           [[0, 128], [1, 512]])))

                    @blk.vector
                    def _(v):
                        for qb in range(8):
                            LPK.wait(v, lpk0 + qb)
                            if qb >= 2:
                                SPS.wait(v, stores[qb - 2])
                            elif hl >= 1:
                                SPS.wait(v, n_store_idx[hl - 1][qb - 2])
                            cnt = ve(v.tensor_mul(STG[2 + qb % 2][0:64, :],
                                                  OSTG_F[hb][0:64, 512 * qb:512 * qb + 512],
                                                  SLOT(10 + qb % 2)[0:64, :]))
                            assert cnt == ve_base + qb + 1
                n_lpk_bcast[hl] = LPK.n - 1
                n_store_idx[hl] = stores

            for hl in range(8):
                pb = hl % 2
                with nc.Block() as blk:
                    lr0 = LPR.n

                    @blk.gpsimd
                    def _(g):
                        if hl < 2:
                            g.wait_ge(s_pe, pe_fcum)
                            SPS.wait(g, fq_sps)
                            SPS.wait(g, fq_sps - 1)
                            SPS.wait(g, kqv["kt"])
                            SPS.wait(g, kqv["q"])
                            SPS.wait(g, kqv["v"])
                        else:
                            g.wait_ge(s_pe, head_pe_done[hl - 2])
                        LPR.issue(g.dma_start(
                            KTH[pb][0:64, 0:4096].rearrange("p (r t) -> p r t", r=4),
                            bass.AP(kt_my, (64 * hl) * SL, [[SL, 64], [512 * SL, 4], [1, SL]])))
                        LPR.issue(g.dma_start(
                            QTA[pb][0:64, 0:4096].rearrange("p (r t) -> p r t", r=4),
                            bass.AP(q_my, (64 * hl) * SL, [[SL, 64], [512 * SL, 4], [1, SL]])))
                        LPR.issue(g.dma_start(QTA[pb][64:67, 0:4096],
                                              bass.AP(fq_d, hl * S, [[8 * S, 3], [1, S]])))
                        if hl % 2 == 0:
                            LPR.issue(g.dma_start(
                                VPB[(hl // 2) % 2].rearrange("p (t e) -> p t e", t=32),
                                bass.AP(v_my, 130 * (hl // 2),
                                        [[520, 128], [128 * 520, 32], [1, 130]])))
                            vpb_lpr[hl // 2] = LPR.n - 1
                    head_lpr[hl] = (lr0, LPR.n)

                if hl >= 1:
                    emit_norm(hl - 1)

                vpb = VPB[(hl // 2) % 2]
                for qb in range(8):
                    nkc = 4 * qb + 4
                    with nc.Block() as blk:
                        se_base = C["se"]
                        prev_exp = dict(bank_exp)
                        prev_pv = dict(bank_pv)
                        ot_gate = bank_ot[qb % 2]
                        mm_sc = {}
                        mm_pv = {}
                        mask_ve = {}
                        spacer_pe = [0]

                        @blk.tensor
                        def _(t_):
                            if qb == 0:
                                for i in range(head_lpr[hl][0], head_lpr[hl][1]):
                                    LPR.wait(t_, i)
                                if hl % 2 == 1:
                                    LPR.wait(t_, vpb_lpr[hl // 2])
                                t_.wait_ge(s_ve, ve_ones)

                            def pv(kcp):
                                t_.wait_ge(s_se, se_base + kcp + 1)
                                if kcp == 0 and ot_gate > 0:
                                    t_.wait_ge(s_ve, ot_gate)
                                mi = t_.matmul(PS[4 + qb % 2][0:65, :],
                                               vpb[:, 130 * kcp + 65 * (hl % 2):
                                                   130 * kcp + 65 * (hl % 2) + 65],
                                               PTB[kcp % 2][:, 0:512],
                                               start=(kcp == 0), stop=(kcp == nkc - 1),
                                               skip_group_check=True)
                                mm_pv[kcp] = pe(mi)

                            for kc in range(nkc):
                                if kc >= 2:
                                    t_.wait_ge(s_se, se_base + kc - 1)
                                elif prev_exp[kc % 2] > 0:
                                    t_.wait_ge(s_se, prev_exp[kc % 2])
                                mi = t_.matmul(PS[kc % 2][:, :],
                                               KTH[pb][0:67, 128 * kc:128 * kc + 128],
                                               QTA[pb][0:67, 512 * qb:512 * qb + 512],
                                               start=True, stop=True, skip_group_check=True)
                                mm_sc[kc] = pe(mi)
                                if kc >= 2:
                                    pv(kc - 2)
                            pv(nkc - 2)
                            pv(nkc - 1)
                            mi = t_.matmul(PS[2][0:1, :], ONESC[:], STG[0][:],
                                           start=True, stop=True)
                            spacer_pe[0] = pe(mi)

                        @blk.vector
                        def _(v):
                            for kc in range(4 * qb, nkc):
                                v.wait_ge(s_pe, mm_sc[kc])
                                mask_ve[kc] = ve(v.tensor_add(PS[kc % 2][:, :], PS[kc % 2][:, :],
                                                              SLOT(kc % 4)[:]))
                            v.wait_ge(s_pe, spacer_pe[0])
                            v.tensor_copy(OSTG_F[hl % 2][0:64, 512 * qb:512 * qb + 512],
                                          PS[4 + qb % 2][0:64, :])
                            bank_ot[qb % 2] = ve(v.reciprocal(
                                LRR[hl % 2][0:1, 512 * qb:512 * qb + 512],
                                PS[4 + qb % 2][64:65, :]))

                        @blk.scalar
                        def _(s):
                            for kc in range(nkc):
                                s.wait_ge(s_pe, mm_sc[kc])
                                if kc >= 4 * qb:
                                    s.wait_ge(s_ve, mask_ve[kc])
                                if kc >= 2:
                                    s.wait_ge(s_pe, mm_pv[kc - 2])
                                elif prev_pv[kc % 2] > 0:
                                    s.wait_ge(s_pe, prev_pv[kc % 2])
                                cnt = se(s.activation(PTB[kc % 2][:, 0:512], PS[kc % 2][:, :],
                                                      AF.Exp,
                                                      bias=FTN[:, 8 * kc + hl:8 * kc + hl + 1]))
                                assert cnt == se_base + kc + 1
                                bank_exp[kc % 2] = cnt
                    for kcp in range(nkc):
                        bank_pv[kcp % 2] = mm_pv[kcp]
                head_pe_done[hl] = C["pe"]
            emit_norm(7)

            with nc.Block() as blk:
                @blk.gpsimd
                def _(g):
                    SPS.wait_all(g)
                    grp = [[0, 1, 2, 3], [4, 5, 6, 7]]
                    cc(g.collective_compute("AllGather", ALU.bypass, replica_groups=grp,
                                            ins=[o_in.ap().opt()], outs=[o_out.ap().opt()]))
            return C["cc"]

        def emit_oproj(cc_o):
            ocp = [0]
            with nc.Block() as blk:
                @blk.gpsimd
                def _(g):
                    g.wait_ge(s_cc, cc_o)
                    pid = nc.partition_id(engines=[mybir.EngineType.Pool])
                    jj = pid % 4
                    ocp[0] = SPS.issue(g.dma_start(
                        o_my[:, :], bass.AP(o_out, jj * 1024, [[S, D], [1, SL]])))

            for h in range(2):
                with nc.Block() as blk:
                    lro = LPR.n
                    ld3 = LPW.n
                    vb3 = C["ve"]
                    peb4 = C["pe"]

                    @blk.gpsimd
                    def _(g):
                        SPS.wait(g, ocp[0])
                        for ee in range(16):
                            LPR.issue(g.dma_start(
                                OT(ee),
                                bass.AP(o_my, 512 * h + (128 * ee) * SL,
                                        [[SL, 128], [1, 512]])))
                        for dd in range(16):
                            if dd >= 2:
                                g.wait_ge(s_pe, peb4 + dd - 1)
                            LPW.issue(g.dma_start(
                                WSB[dd % 2].rearrange("p (t e) -> p t e", t=16),
                                wo.rearrange("(t p) e -> p t e", p=128)[:, :, 128 * dd:128 * dd + 128]))
                            if dd >= 4:
                                g.wait_ge(s_ve, vb3 + dd - 3)
                                SPS.issue(g.dma_start(rx_d[128 * (dd - 4):128 * (dd - 4) + 128,
                                                           512 * h:512 * h + 512],
                                                      RXT[(dd - 4) % 4][:]))
                                SPS.wait(g, SPS.n - 1)
                            LPR.issue(g.dma_start(RXT[dd % 4][:],
                                                  rx_d[128 * dd:128 * dd + 128,
                                                       512 * h:512 * h + 512]))
                        for dd in range(12, 16):
                            g.wait_ge(s_ve, vb3 + dd + 1)
                            SPS.issue(g.dma_start(rx_d[128 * dd:128 * dd + 128,
                                                       512 * h:512 * h + 512],
                                                  RXT[dd % 4][:]))

                    @blk.tensor
                    def _(t_):
                        for ee in range(16):
                            LPR.wait(t_, lro + ee)
                        for dd in range(16):
                            LPW.wait(t_, ld3 + dd)
                            if dd >= 2:
                                t_.wait_ge(s_ve, vb3 + dd - 1)
                            for e in range(16):
                                mi = t_.matmul(PS[2 + dd % 2][:, :],
                                               WSB[dd % 2][:, 128 * e:128 * e + 128],
                                               OT(e), start=(e == 0), stop=(e == 15))
                            pe(mi)

                    @blk.vector
                    def _(v):
                        for dd in range(16):
                            v.wait_ge(s_pe, peb4 + dd + 1)
                            LPR.wait(v, lro + 16 + dd)
                            ve(v.tensor_add(RXT[dd % 4][:], RXT[dd % 4][:], PS[2 + dd % 2][:, :]))

        def emit_ffn(h):
            cu_slot = (2, 9)[h]
            with nc.Block() as blk:
                ld0 = LPW.n
                vb = C["ve"]
                peb = C["pe"]

                @blk.gpsimd
                def _(g):
                    for fc in range(64):
                        if fc >= 2:
                            g.wait_ge(s_pe, peb + fc - 1)
                        LPW.issue(g.dma_start(
                            QGB[fc % 2].rearrange("p (t f) -> p t f", t=16),
                            qg.rearrange("(t p) f -> p t f", p=128)[:, :, 128 * fc:128 * fc + 128]))

                @blk.tensor
                def _(t_):
                    t_.wait_ge(s_ve, vb)
                    for fc in range(64):
                        LPW.wait(t_, ld0 + fc)
                        if fc >= 2:
                            t_.wait_ge(s_ve, vb + 3 * (fc - 2) + 1)
                        for d in range(16):
                            mi = t_.matmul(PS[fc % 2][:, :], QGB[fc % 2][:, 128 * d:128 * d + 128],
                                           XN(d, h), start=(d == 0), stop=(d == 15))
                        pe(mi)

                @blk.vector
                def _(v):
                    for fc in range(64):
                        v.wait_ge(s_pe, peb + fc + 1)
                        ve(v.scalar_tensor_tensor(SLOT(14)[:], PS[fc % 2][:, :], GCOL[:, 3:4],
                                                  SLOT(cu_slot)[:], ALU.mult, ALU.mult))
                        ve(v.tensor_scalar_max(SLOT(7)[:], SLOT(14)[:], 0.0))
                        ve(v.tensor_mul(HT(fc), SLOT(14)[:], SLOT(7)[:]))

            with nc.Block() as blk:
                seb3 = C["se"]
                veb3 = C["ve"]

                @blk.scalar
                def _(s):
                    for fc in range(64):
                        if fc >= 2:
                            s.wait_ge(s_ve, veb3 + fc - 1)
                        se(s.activation(SQT[fc % 2][:], HT(fc), AF.Abs))

                @blk.vector
                def _(v):
                    v.memset(AMB[:, 0:512], 0.0)
                    for fc in range(64):
                        v.wait_ge(s_se, seb3 + fc + 1)
                        ve(v.tensor_max(AMB[:, 0:512], AMB[:, 0:512], SQT[fc % 2][:]))

            emit_tokscale(512, (3,), (4,), (5,))

            with nc.Block() as blk:
                @blk.vector
                def _(v):
                    LPK.wait_all(v)
                    for fc in range(64):
                        v.tensor_mul(SLOT(14)[:], HT(fc), SLOT(4)[:])
                        v.tensor_scalar_add(SLOT(14)[:], SLOT(14)[:], 0.5)
                        v.tensor_copy(ITL[:], SLOT(14)[:])
                        v.tensor_copy(SLOT(13)[:], ITL[:])
                        v.tensor_sub(SLOT(7)[:], SLOT(13)[:], SLOT(14)[:])
                        v.tensor_scalar(SLOT(7)[:], SLOT(7)[:], 1e30, 1.0, ALU.mult, ALU.min)
                        v.tensor_scalar_max(SLOT(7)[:], SLOT(7)[:], 0.0)
                        ve(v.tensor_sub(HT(fc), SLOT(13)[:], SLOT(7)[:]))

            with nc.Block() as blk:
                ld1 = LPW.n
                ldr = LPR.n
                vb3 = C["ve"]
                peb2 = C["pe"]

                @blk.gpsimd
                def _(g):
                    g.wait_ge(s_ve, C["ve"])
                    for dd in range(16):
                        if dd >= 2:
                            g.wait_ge(s_pe, peb2 + dd - 1)
                        LPW.issue(g.dma_start(
                            QDB[dd % 2].rearrange("p (t f) -> p t f", t=64),
                            qd.rearrange("(t p) f -> p t f", p=128)[:, :, 128 * dd:128 * dd + 128]))
                        if dd >= 4:
                            g.wait_ge(s_ve, vb3 + 2 * (dd - 3))
                            SPS.issue(g.dma_start(rx_d[128 * (dd - 4):128 * (dd - 4) + 128,
                                                       512 * h:512 * h + 512],
                                                  RXT[(dd - 4) % 4][:]))
                            SPS.wait(g, SPS.n - 1)
                        LPR.issue(g.dma_start(RXT[dd % 4][:],
                                              rx_d[128 * dd:128 * dd + 128, 512 * h:512 * h + 512]))
                    for dd in range(12, 16):
                        g.wait_ge(s_ve, vb3 + 2 * dd + 2)
                        SPS.issue(g.dma_start(rx_d[128 * dd:128 * dd + 128, 512 * h:512 * h + 512],
                                              RXT[dd % 4][:]))

                @blk.tensor
                def _(t_):
                    t_.wait_ge(s_ve, vb3)
                    for dd in range(16):
                        LPW.wait(t_, ld1 + dd)
                        if dd >= 2:
                            t_.wait_ge(s_ve, vb3 + 2 * (dd - 2) + 1)
                        for fc in range(64):
                            mi = t_.matmul(PS[6 + dd % 2][:, :], QDB[dd % 2][:, 128 * fc:128 * fc + 128],
                                           HT(fc), start=(fc == 0), stop=(fc == 63))
                        pe(mi)

                @blk.vector
                def _(v):
                    for dd in range(16):
                        v.wait_ge(s_pe, peb2 + dd + 1)
                        LPR.wait(v, ldr + dd)
                        ve(v.scalar_tensor_tensor(SLOT(14)[:], PS[6 + dd % 2][:, :], GCOL[:, 4:5],
                                                  SLOT(5)[:], ALU.mult, ALU.mult))
                        ve(v.tensor_add(RXT[dd % 4][:], RXT[dd % 4][:], SLOT(14)[:]))

        emit_ln(False)
        if ph >= 2:
            emit_projq(wq, q_in, scale=0.125)
            emit_projq(wk, kt_in)
            emit_projv()
            emit_projf()
            if ph < 9 and os.environ.get("L1_XNDUMP"):
                dbg["xn"] = nc.dram_tensor("dbg_xn", [128, 16384], F32,
                                           kind="ExternalOutput")
                with nc.Block() as blk:
                    pe_cap2 = C["pe"]
                    ve_cap2 = C["ve"]

                    @blk.gpsimd
                    def _(g):
                        g.wait_ge(s_pe, pe_cap2)
                        g.wait_ge(s_ve, ve_cap2)
                        SPS.issue(g.dma_start(dbg["xn"][:, :], A1[:]))
            cc0 = C["cc"]
            emit_gather()
            cc_f = cc0 + 1
            cc_all = cc0 + 4
        if ph >= 3:
            f_idx = emit_fcopy(cc_f)
            fq_sps = emit_fcum(f_idx)
            kqv = emit_kqv(cc0)
        if ph >= 4:
            cc_o = emit_attn1(kqv, fq_sps)
            emit_oproj(cc_o)
        if ph >= 5:
            emit_ln(True)
        if ph >= 9:
            for h in range(2):
                emit_ffn(h)

        with nc.Block() as blk:
            @blk.gpsimd
            def _(g):
                g.wait_ge(s_ve, C["ve"])
                g.wait_ge(s_se, C["se"])
                g.wait_ge(s_pe, C["pe"])
                SPS.wait_all(g)
                SPS.issue(g.dma_start(yt[:, :], rx_d[:, :]))
                if ph < 9:
                    SPS.issue(g.dma_start(dbg["rx"][:, :], rx_d[:, :]))
                    SPS.issue(g.dma_start(dbg["a1"][:, :], A1[:]))
                    SPS.issue(g.dma_start(dbg["a2"][:, :], A2[:]))
                    SPS.issue(g.dma_start(dbg["a3"][:, :], A3[:]))
                    SPS.issue(g.dma_start(dbg["fc"][0:8, :], fcum_d[:, :]))
                    SPS.issue(g.dma_start(dbg["ftn"][:, :], FTN[:]))
                    if ph >= 2:
                        SPS.issue(g.dma_start(dbg["q"][:, :], q_in[:, :]))
                        SPS.issue(g.dma_start(dbg["k"][:, :], kt_in[:, :]))
                        SPS.issue(g.dma_start(dbg["v"][:, :], v_in[:, :]))
                        SPS.issue(g.dma_start(dbg["f"][:, :], f_in[:, :]))
                    if ph >= 4:
                        SPS.issue(g.dma_start(dbg["o"][:, :], o_out[:, :]))
                SPS.wait_all(g)

    return nc
def _build_msk():
    m = np.zeros((128, 2048), np.float32)
    for v in range(4):
        km = np.arange(128)[:, None] + 128 * v
        qm = np.arange(512)[None, :]
        m[:, 512 * v:512 * v + 512] = np.where(qm >= km, 0.0, -1e30)
    return m


def kernel(**inputs):
    x = inputs["L0_attn_x"].astype(np.float32)
    cores = list(range(NC))

    gT = np.ascontiguousarray(inputs["L0_ffn_w_gate"].T)
    uT = np.ascontiguousarray(inputs["L0_ffn_w_up"].T)
    dT = np.ascontiguousarray(inputs["L0_ffn_w_down"].T)
    w1T = np.ascontiguousarray(inputs["L1_ffn_w1"].T)
    w2T = np.ascontiguousarray(inputs["L1_ffn_w2"].T)
    maps1 = []
    for c in cores:
        maps1.append({
            "w_g": np.ascontiguousarray(gT[:, c * 1024:(c + 1) * 1024]),
            "w_u": np.ascontiguousarray(uT[:, c * 1024:(c + 1) * 1024]),
            "w_d": np.ascontiguousarray(dT[c * 1024:(c + 1) * 1024, :]),
            "w_w1": np.ascontiguousarray(w1T[:, c * 1024:(c + 1) * 1024]),
            "w_w2": np.ascontiguousarray(w2T[c * 1024:(c + 1) * 1024, :]),
        })
    r1 = run_bass_kernel_spmd(build_prep(), maps1, core_ids=cores)
    qg = np.concatenate([r1.results[c]["q_g"] for c in cores], axis=1)
    qu = np.concatenate([r1.results[c]["q_u"] for c in cores], axis=1)
    qd = np.concatenate([r1.results[c]["q_d"] for c in cores], axis=0)
    qw1 = np.concatenate([r1.results[c]["q_w1"] for c in cores], axis=1)
    qw2 = np.concatenate([r1.results[c]["q_w2"] for c in cores], axis=0)
    gam = np.asarray(r1.results[0]["gammas"])

    maps2 = []
    for c in cores:
        bat, j = c // 4, c % 4
        maps2.append({
            "xt": np.ascontiguousarray(x[bat][1024 * j:1024 * j + 1024].T),
            "wq": inputs["L0_attn_W_q"], "wk": inputs["L0_attn_W_k"],
            "wv": inputs["L0_attn_W_v"], "wo": inputs["L0_attn_W_o"],
            "qg": np.ascontiguousarray(qg), "qu": np.ascontiguousarray(qu),
            "qd": np.ascontiguousarray(qd), "gam": gam,
        })
    r2 = run_bass_kernel_spmd(build_l0(0), maps2, core_ids=cores)

    wq1t = np.ascontiguousarray(inputs["L1_attn_Wq"].T)
    wk1t = np.ascontiguousarray(inputs["L1_attn_Wk"].T)
    wv1t = np.ascontiguousarray(inputs["L1_attn_Wv"].T)
    wo1t = np.ascontiguousarray(inputs["L1_attn_Wo"].T)
    wf1t = np.ascontiguousarray(inputs["L1_attn_Wf"].T)
    bf1 = np.ascontiguousarray(inputs["L1_attn_bf"].reshape(32, 1).astype(np.float32))
    id128 = np.eye(128, dtype=np.float32)
    msk = _build_msk()
    maps3 = []
    for c in cores:
        maps3.append({
            "xt": np.ascontiguousarray(np.asarray(r2.results[c]["yt"])),
            "wq1t": wq1t, "wk1t": wk1t, "wv1t": wv1t, "wo1t": wo1t,
            "wf1t": wf1t, "bf1": bf1,
            "qw1": np.ascontiguousarray(qw1), "qw2": np.ascontiguousarray(qw2),
            "gam": gam, "id128": id128, "msk": msk,
        })
    r3 = run_bass_kernel_spmd(build_l1(0), maps3, core_ids=cores)

    out = np.empty((B, S, D), np.float32)
    for c in cores:
        bat, j = c // 4, c % 4
        out[bat][1024 * j:1024 * j + 1024] = np.asarray(r3.results[c]["yt"]).T
    return out
